# version 21
# baseline (speedup 1.0000x reference)
import contextlib
import numpy as np
import concourse.bass as bass
import concourse.mybir as mybir
from concourse.bass_utils import run_bass_kernel_spmd

F32 = mybir.dt.float32
BF16 = mybir.dt.bfloat16
AF = mybir.ActivationFunctionType
ALU = mybir.AluOpType

HID = 1024
NH = 8
DK = 128
CH = 8
SCALE = float(DK) ** -0.5
EPS = 1e-5

ENGS = ("tensor", "vector", "scalar", "gpsimd", "sync")
ENG_SEM = {"tensor": "T", "vector": "V", "scalar": "A", "gpsimd": "G", "sync": "S"}


class Buf:
    __slots__ = ("w", "readers", "track", "excl")

    def __init__(self, track=True, excl=False):
        self.w = None
        self.readers = []
        self.track = track
        self.excl = excl


class Op:
    __slots__ = ("oid", "eng", "fn", "deps", "semn", "inc", "est", "cnt",
                 "issue")

    def __init__(self, oid, eng, fn, deps, semn, inc, est, issue):
        self.oid = oid
        self.eng = eng
        self.fn = fn
        self.deps = deps
        self.semn = semn
        self.inc = inc
        self.est = est
        self.issue = issue
        self.cnt = None


HOP_NS = 300.0

_DEF_EST = {"tensor": 1500.0, "vector": 900.0, "scalar": 1100.0,
            "gpsimd": 1500.0, "sync": 600.0}


class Sched:
    def __init__(self, sems):
        self.sems = sems
        self.cnt = {n: 0 for n in sems}
        self.ops = []
        self.last = {n: None for n in sems}
        self._oid = 0

    def op(self, eng, fn, reads=(), writes=(), sem=None, dma=False, extra=(),
           est=None):
        semn = sem or ENG_SEM[eng]
        inc = 16 if dma else 1
        if est is None:
            est = 3000.0 if dma else _DEF_EST[eng]
        issue = 300.0 if dma else est
        reads = list(reads)
        writes = list(writes)
        moved = [b for b in reads if b.excl]
        if moved:
            reads = [b for b in reads if not b.excl]
            writes = writes + [b for b in moved if b not in writes]
        deps = {}
        for b in reads:
            if b.w is not None:
                deps[b.w.oid] = b.w
        for b in writes:
            if b.w is not None:
                deps[b.w.oid] = b.w
            for r in b.readers:
                deps[r.oid] = r
        for e in extra:
            if e is not None:
                deps[e.oid] = e
        if dma and self.last[semn] is not None:
            lo = self.last[semn]
            deps[lo.oid] = lo
        o = Op(self._oid, eng, fn, list(deps.values()), semn, inc, est, issue)
        self._oid += 1
        self.cnt[semn] += inc
        self.last[semn] = o
        self.ops.append(o)
        for b in reads:
            if b.track:
                b.readers.append(o)
        for b in writes:
            b.w = o
            b.readers = []
        return o

    def pin(self, semn, bufs):
        lo = self.last[semn]
        for b in bufs:
            if b.w is not None and b.w.semn == semn:
                b.w = lo
            b.readers = [lo if (r.semn == semn) else r for r in b.readers]
        return lo

    def flush(self, block):
        ops = self.ops
        self.ops = []
        if not ops:
            return
        idx = {o.oid: o for o in ops}
        succ = {o.oid: [] for o in ops}
        ndep = {o.oid: 0 for o in ops}
        for o in ops:
            for d in o.deps:
                if d.oid in idx:
                    succ[d.oid].append(o)
                    ndep[o.oid] += 1
        cp = {}
        for o in reversed(ops):
            m = 0.0
            for s2 in succ[o.oid]:
                m = max(m, cp[s2.oid])
            cp[o.oid] = m + o.est
        import heapq
        engs = sorted({o.eng for o in ops})
        readyq = {e: [] for e in engs}
        rtime = {}
        done_t = {}
        eng_t = {e: 0.0 for e in engs}
        seq = {e: [] for e in engs}

        def push_ready(o):
            rt = 0.0
            for d in o.deps:
                if d.oid in done_t:
                    rt = max(rt, done_t[d.oid] +
                             (HOP_NS if d.eng != o.eng else 0.0))
            rtime[o.oid] = rt
            heapq.heappush(readyq[o.eng], (-cp[o.oid], o.oid))

        for o in ops:
            if ndep[o.oid] == 0:
                push_ready(o)
        n_left = len(ops)
        while n_left:
            best = None
            for e in engs:
                if readyq[e]:
                    _, oid = readyq[e][0]
                    o = idx[oid]
                    st = max(eng_t[e], rtime[oid])
                    key = (st, -cp[oid])
                    if best is None or key < best[0]:
                        best = (key, e)
            _, e = best
            _, oid = heapq.heappop(readyq[e])
            o = idx[oid]
            st = max(eng_t[e], rtime[oid])
            done_t[oid] = st + o.est
            eng_t[e] = st + o.issue
            seq[e].append(o)
            n_left -= 1
            for s2 in succ[oid]:
                ndep[s2.oid] -= 1
                if ndep[s2.oid] == 0:
                    push_ready(s2)

        cnt = {n: 0 for n in self.sems}
        for n in self.sems:
            cnt[n] = self._base.get(n, 0) if hasattr(self, "_base") else 0
        if not hasattr(self, "_base"):
            self._base = {}
        for e in engs:
            for o in seq[e]:
                cnt[o.semn] += o.inc
                o.cnt = cnt[o.semn]
        self._base = dict(cnt)

        for e in engs:
            if not seq[e]:
                continue

            def body(h, e=e, olist=seq[e]):
                waited = {}
                for o in olist:
                    ws = {}
                    for d in o.deps:
                        assert d.cnt is not None, "dep scheduled later!"
                        if ws.get(d.semn, 0) < d.cnt:
                            ws[d.semn] = d.cnt
                    for wsem in sorted(ws):
                        wval = ws[wsem]
                        if waited.get(wsem, 0) < wval:
                            h.wait_ge(self.sems[wsem], wval)
                            waited[wsem] = wval
                    inst = o.fn(h)
                    if inst is None:
                        inst = h.nop()
                    inst.then_inc(self.sems[o.semn], o.inc)

            getattr(block, e)(body)


def slots(p, n, k):
    return p.ap().rearrange("p (g t) -> p g t", t=n)[:, 0:k, :]


def build_program(HALF=4096, V2=False, n_main=None, do_mini=True, mini_stage=99, dbg=False):
    TKV = 512
    TM = 256
    NKV = (2 * HALF) // TKV if not V2 else HALF // TKV
    SNAP = (HALF // TKV - 1) if not V2 else (NKV - 1)
    NM = HALF // TM
    NTB = HALF // 128
    NTBK = TKV // 128
    XTOT = 2 * HALF if not V2 else HALF
    MOFF = XTOT - HALF

    nc = bass.Bass("TRN2", target_bir_lowering=False)

    x_h = nc.dram_tensor("xkv", [HID, XTOT], F32, kind="ExternalInput")
    w_in = {}
    for nm in ("wq", "wk", "wv", "wo", "wr", "wk2", "wv2"):
        w_in[nm] = nc.dram_tensor(nm, [HID, HID], F32, kind="ExternalInput")
    for nm in ("fmq1", "fmq2", "fmk1", "fmk2"):
        w_in[nm] = nc.dram_tensor(nm, [DK, DK], F32, kind="ExternalInput")
    vec_in = {}
    for nm in ("fbq1", "fbq2", "fbk1", "fbk2", "rmsw"):
        vec_in[nm] = nc.dram_tensor(nm, [DK], F32, kind="ExternalInput")
    mur_h = nc.dram_tensor("mur", [128, CH], F32, kind="ExternalInput")
    muk_h = nc.dram_tensor("muk", [128, CH], F32, kind="ExternalInput")
    ident_h = nc.dram_tensor("ident", [128, 128], F32, kind="ExternalInput")
    trimask_h = nc.dram_tensor("trimask", [128, 256], F32, kind="ExternalInput")
    y_h = nc.dram_tensor("y", [HALF, HID], F32, kind="ExternalOutput")
    if dbg:
        d_outs = {
            "d_ln1": nc.dram_tensor("d_ln1", [CH, 128, HALF], BF16,
                                    kind="ExternalOutput"),
            "d_kf": nc.dram_tensor("d_kf", [NH, 128, HALF], BF16,
                                   kind="ExternalOutput"),
            "d_kt": nc.dram_tensor("d_kt", [HALF // 128, 128, NH, DK], BF16,
                                   kind="ExternalOutput"),
            "d_v": nc.dram_tensor("d_v", [HALF // 128, 128, NH, DK], BF16,
                                  kind="ExternalOutput"),
            "d_sf": nc.dram_tensor("d_sf", [128, NH * DK], F32,
                                   kind="ExternalOutput"),
            "d_h2c": nc.dram_tensor("d_h2c", [128, CH * 2], F32,
                                    kind="ExternalOutput"),
            "d_m": nc.dram_tensor("d_m", [128, 512], F32,
                                  kind="ExternalOutput"),
            "d_a": nc.dram_tensor("d_a", [128, 512], F32,
                                  kind="ExternalOutput"),
            "d_xb": nc.dram_tensor("d_xb", [128, CH * 512], F32,
                                   kind="ExternalOutput"),
            "d_sq": nc.dram_tensor("d_sq", [128, CH * 512], F32,
                                   kind="ExternalOutput"),
        }
    s_ln1 = nc.dram_tensor("s_ln1", [CH, 128, HALF], BF16)
    s_kf = nc.dram_tensor("s_kf", [NH, 128, HALF], BF16)
    s_kt = nc.dram_tensor("s_kt", [NTB, 128, NH, DK], BF16)
    s_v = nc.dram_tensor("s_v", [NTB, 128, NH, DK], BF16)
    if V2:
        flg_h = nc.dram_tensor("flags", [128, 2], F32, kind="ExternalInput")
        cc_in = nc.dram_tensor("cc_in", [128, 1032], F32)
        cc_out = nc.dram_tensor("cc_out", [128, 1032], F32)

    with contextlib.ExitStack() as stack:
        ST = stack.enter_context
        sems = {}
        for nm in ("T", "V", "A", "G", "S", "SC", "SX", "L0", "L1",
                   "WA", "WK", "WV", "W2a", "W2b", "W2c", "W2d", "W2e",
                   "ST0", "ST1", "DBG", "DBG2", "K0", "K1", "K2", "K3",
                   "CC", "CC2", "CCI", "LS0", "LS1", "LS2", "SX1",
                   "MX", "MX1"):
            sems[nm] = ST(nc.semaphore("sem" + nm))
        sch = Sched(sems)
        op = sch.op

        ident32 = ST(nc.sbuf_tensor("ident32", [128, 128], F32))
        identb = ST(nc.sbuf_tensor("identb", [128, 128], BF16))
        onesb = ST(nc.sbuf_tensor("onesb", [128, 128], BF16))
        trimask = ST(nc.sbuf_tensor("trimask_s", [128, 256], F32))
        consts = ST(nc.sbuf_tensor("consts", [128, 8], F32))
        tmpv = ST(nc.sbuf_tensor("tmpv", [128, 8], F32))
        vecs = {nm: ST(nc.sbuf_tensor(nm + "_s", [128, 1], F32))
                for nm in ("fbq1", "fbq2s", "fbk1", "fbk2", "rmsw")}
        mur = ST(nc.sbuf_tensor("mur_s", [128, CH], BF16))
        muk = ST(nc.sbuf_tensor("muk_s", [128, CH], BF16))
        fmw = {nm: ST(nc.sbuf_tensor(nm + "_s", [128, 128], BF16))
               for nm in ("fmq1", "fmq2", "fmk1", "fmk2")}
        wmain = {nm: ST(nc.sbuf_tensor(nm + "_s", [128, CH, HID], BF16))
                 for nm in ("wq", "wo", "wr", "wk2", "wv2")}
        Sf = ST(nc.sbuf_tensor("Sf", [128, NH, DK], F32))
        Sb = ST(nc.sbuf_tensor("Sb", [128, NH, DK], BF16))
        h2c = ST(nc.sbuf_tensor("h2c", [128, CH, 2], BF16))
        if V2:
            sflag = ST(nc.sbuf_tensor("sflag", [128, 1], F32))
            uflag = ST(nc.sbuf_tensor("uflag", [128, 1], F32))

        B = {}

        def buf(name, track=True, excl=False):
            B[name] = Buf(track, excl)
            return B[name]

        for nm in ("ident32", "identb", "onesb", "trimask", "mur", "muk",
                   "fmw"):
            buf(nm, track=False)
        for nm in ("consts", "vecs", "tmpv", "Sf",
                   "Sb", "h2c", "wmain", "wkv", "s_ln1", "s_kf", "s_kt",
                   "s_v", "flags"):
            buf(nm)

        c_eps = consts[:, 0:1]
        c_nh = consts[:, 1:2]
        c_n1 = consts[:, 2:3]
        c_i128 = consts[:, 3:4]
        c_one = consts[:, 4:5]

        op("sync", lambda h: h.dma_start(out=ident32[:, :], in_=ident_h.ap()),
           writes=[B["ident32"]], sem="SC", dma=True)
        op("sync", lambda h: h.dma_start(out=trimask[:, :], in_=trimask_h.ap()),
           writes=[B["trimask"]], sem="SC", dma=True)
        op("gpsimd", lambda h: h.dma_start(out=mur[:, :], in_=mur_h.ap()),
           writes=[B["mur"]], sem="WA", dma=True)
        op("gpsimd", lambda h: h.dma_start(out=muk[:, :], in_=muk_h.ap()),
           writes=[B["muk"]], sem="WA", dma=True)
        if V2:
            op("sync", lambda h: h.dma_start(
                   out=sflag[:, 0:1],
                   in_=flg_h.ap()[:, 0:1],
                   allow_slow_non_contiguous=True),
               writes=[B["flags"]], sem="SC", dma=True)
            op("sync", lambda h: h.dma_start(
                   out=uflag[:, 0:1],
                   in_=flg_h.ap()[:, 1:2],
                   allow_slow_non_contiguous=True),
               writes=[B["flags"]], sem="SC", dma=True)
        op("gpsimd", lambda h: h.dma_start(out=identb[:, :], in_=ident_h.ap()),
           writes=[B["identb"]], sem="WA", dma=True)
        op("gpsimd", lambda h: h.memset(onesb[:, :], 1.0), writes=[B["onesb"]])
        op("gpsimd", lambda h: h.memset(consts[:, 0:1], EPS), writes=[B["consts"]])
        op("gpsimd", lambda h: h.memset(consts[:, 1:2], -0.5), writes=[B["consts"]])
        op("gpsimd", lambda h: h.memset(consts[:, 2:3], -1.0), writes=[B["consts"]])
        op("gpsimd", lambda h: h.memset(consts[:, 3:4], 1.0 / DK),
           writes=[B["consts"]])
        op("gpsimd", lambda h: h.memset(consts[:, 4:5], 1.0),
           writes=[B["consts"]])
        for i, nm in enumerate(("fbq1", "fbq2", "fbk1", "fbk2", "rmsw")):
            op("sync",
               lambda h, i=i, nm=nm: h.dma_start(out=tmpv[:, i:i + 1],
                                                 in_=vec_in[nm].ap()[:, None]),
               writes=[B["tmpv"]], sem="SC", dma=True)
        sch.pin("SC", [B["ident32"], B["trimask"], B["mur"], B["muk"],
                       B["tmpv"]] + ([B["flags"]] if V2 else []))
        op("vector", lambda h: h.tensor_scalar_mul(vecs["fbq2s"][:, 0:1],
                                                   tmpv[:, 1:2], SCALE),
           reads=[B["tmpv"]], writes=[B["vecs"]])
        for i, nm in ((0, "fbq1"), (2, "fbk1"), (3, "fbk2"), (4, "rmsw")):
            op("vector", lambda h, i=i, nm=nm: h.tensor_copy(vecs[nm][:, 0:1],
                                                             tmpv[:, i:i + 1]),
               reads=[B["tmpv"]], writes=[B["vecs"]])
        for nm in ("fmq1", "fmq2", "fmk1", "fmk2"):
            op("gpsimd", lambda h, nm=nm: h.dma_start(out=fmw[nm][:, :],
                                                      in_=w_in[nm].ap()),
               writes=[B["fmw"]], sem="WA", dma=True)

        def load_weight(nm, dst, semn, bname):
            src = w_in[nm].ap().rearrange("(k p) d -> p k d", p=128)
            op("gpsimd", lambda h: h.dma_start(out=dst[:, :, :], in_=src),
               writes=[B[bname]], sem=semn, dma=True)

        def ln_stats_ops(xb_sl, sq_sl, xbB, sqB, ps_s, psB_s, ps_q, psB_q,
                         mb, vb, ab, mB, vB, aB, seq_sumsq=False,
                         split_mm=False):
            sqBs = sqB if isinstance(sqB, list) else [sqB]
            if split_mm:
                def summ(h):
                    inst = None
                    for kc in range(CH):
                        inst = h.matmul(ps_s, onesb[:, :], xb_sl[kc],
                                        start=(kc == 0), stop=(kc == CH - 1))
                    return inst
                op("tensor", summ, reads=[xbB, B["onesb"]], writes=[psB_s])
                op("vector", lambda h: h.tensor_scalar_mul(mb, ps_s, 1.0 / HID),
                   reads=[psB_s], writes=[mB])
                def sqm(h):
                    inst = None
                    for kc in range(CH):
                        inst = h.matmul(ps_q, onesb[:, :], sq_sl[kc],
                                        start=(kc == 0), stop=(kc == CH - 1))
                    return inst
                op("tensor", sqm, reads=sqBs + [B["onesb"]], writes=[psB_q])
            elif seq_sumsq:
                def mms1(h):
                    inst = None
                    for kc in range(CH):
                        inst = h.matmul(ps_s, onesb[:, :], xb_sl[kc],
                                        start=(kc == 0), stop=(kc == CH - 1))
                    return inst
                op("tensor", mms1, reads=[xbB, B["onesb"]], writes=[psB_s])
                op("vector", lambda h: h.tensor_scalar_mul(mb, ps_s, 1.0 / HID),
                   reads=[psB_s], writes=[mB])
                def mms2(h):
                    inst = None
                    for kc in range(CH):
                        inst = h.matmul(ps_q, onesb[:, :], sq_sl[kc],
                                        start=(kc == 0), stop=(kc == CH - 1))
                    return inst
                op("tensor", mms2, reads=sqBs + [B["onesb"]], writes=[psB_q])
            else:
                def mms(h):
                    for kc in range(CH):
                        h.matmul(ps_s, onesb[:, :], xb_sl[kc],
                                 start=(kc == 0), stop=(kc == CH - 1))
                    inst = None
                    for kc in range(CH):
                        inst = h.matmul(ps_q, onesb[:, :], sq_sl[kc],
                                        start=(kc == 0), stop=(kc == CH - 1))
                    return inst
                wr = [psB_s] if psB_s is psB_q else [psB_s, psB_q]
                op("tensor", mms, reads=[xbB] + sqBs + [B["onesb"]], writes=wr)
                op("vector", lambda h: h.tensor_scalar_mul(mb, ps_s, 1.0 / HID),
                   reads=[psB_s], writes=[mB])
            op("vector", lambda h: h.tensor_mul(vb, mb, mb),
               reads=[mB], writes=[vB])
            op("vector", lambda h: h.scalar_tensor_tensor(
                   out=vb, in0=ps_q, scalar=1.0 / HID, in1=vb,
                   op0=ALU.mult, op1=ALU.subtract),
               reads=[psB_q, vB], writes=[vB])
            op("scalar", lambda h: h.activation(vb, vb, AF.Ln, bias=c_eps),
               reads=[vB, B["consts"]], writes=[vB])
            op("scalar", lambda h: h.activation(ab, vb, AF.Exp, scale=c_nh),
               reads=[vB, B["consts"]], writes=[aB])

        with contextlib.ExitStack() as kvs:
            KT = kvs.enter_context
            wk = KT(nc.sbuf_tensor("wk_s", [128, CH, HID], BF16))
            wv = KT(nc.sbuf_tensor("wv_s", [128, CH, HID], BF16))
            kxb = KT(nc.sbuf_tensor("kxb", [128, CH, TKV], BF16))
            kf = KT(nc.sbuf_tensor("kf", [128, NH, TKV], BF16))
            ksq = KT(nc.sbuf_tensor("ksq", [128, CH, TKV], BF16))
            kln = KT(nc.sbuf_tensor("kln", [128, 2, CH, TKV], BF16))
            kp = KT(nc.sbuf_tensor("kp", [128, NH, TKV], BF16))
            kt_b = KT(nc.sbuf_tensor("kt_b", [128, NTBK, NH, DK], BF16))
            vt_b = KT(nc.sbuf_tensor("vt_b", [128, NTBK, NH, DK], BF16))
            mbufk = KT(nc.sbuf_tensor("mbufk", [128, TKV], BF16))
            vbufk = KT(nc.sbuf_tensor("vbufk", [128, TKV], F32))
            abufk = KT(nc.sbuf_tensor("abufk", [128, TKV], BF16))
            pkM0 = KT(nc.psum_tensor("pkM0", [128, 1024], F32))
            pkM1 = KT(nc.psum_tensor("pkM1", [128, 1024], F32))
            pkV = KT(nc.psum_tensor("pkV", [128, 1024], F32))
            pkT = KT(nc.psum_tensor("pkT", [128, 1024], BF16))
            pkS = KT(nc.psum_tensor("pkS", [128, 512], F32))
            for nm in ("kxb", "kf", "ksq", "kln0", "kln1", "kp", "kt_b",
                       "vt_b", "mbufk", "vbufk", "abufk"):
                buf(nm)
            for nm in ("pkM0", "pkM1", "pkV", "pkT", "pkS"):
                buf(nm, excl=True)

            load_weight("wk", wk, "WK", "wkv")
            load_weight("wv", wv, "WV", "wkv")

            def gen_kv_front(t):
                x0 = t * TKV
                par = t % 2
                klnv = kln[:, par]
                klnB = B["kln0"] if par == 0 else B["kln1"]
                src = x_h.ap().rearrange("(k p) t -> p k t", p=128)[
                    :, :, x0:x0 + TKV]
                op("gpsimd", lambda h, src=src: h.dma_start(
                       out=kxb[:, :, :], in_=src),
                   writes=[B["kxb"]], sem="MX", dma=True)
                op("scalar", lambda h: h.activation(ksq[:, :, :], kxb[:, :, :],
                                                    AF.Square),
                   reads=[B["kxb"]], writes=[B["ksq"]])
                yield
                ln_stats_ops([kxb[:, kc, :] for kc in range(CH)],
                             [ksq[:, kc, :] for kc in range(CH)],
                             B["kxb"], B["ksq"],
                             pkS[:, 0:TKV], B["pkS"], pkS[:, 0:TKV], B["pkS"],
                             mbufk[:, :], vbufk[:, :], abufk[:, :],
                             B["mbufk"], B["vbufk"], B["abufk"],
                             seq_sumsq=True)
                yield
                mbv = mbufk[:, None, :].broadcast_to([128, CH, TKV])
                abv = abufk[:, None, :].broadcast_to([128, CH, TKV])
                op("vector", lambda h, mbv=mbv, klnv=klnv: h.tensor_sub(
                       klnv, kxb[:, :, :], mbv),
                   reads=[B["kxb"], B["mbufk"]], writes=[klnB])
                op("vector", lambda h, abv=abv, klnv=klnv: h.tensor_mul(
                       klnv, klnv, abv),
                   reads=[B["abufk"], klnB], writes=[klnB])
                yield

            def gen_kv_back(t):
                x0 = t * TKV
                par = t % 2
                klnv = kln[:, par]
                klnB = B["kln0"] if par == 0 else B["kln1"]

                for og in range(4):
                    ps, psB = ((pkM1, B["pkM1"]), (pkM0, B["pkM0"]))[og % 2]
                    def kproj(h, og=og, ps=ps, klnv=klnv):
                        inst = None
                        for j, oc in enumerate((2 * og, 2 * og + 1)):
                            for kc in range(CH):
                                inst = h.matmul(
                                    ps[:, j * TKV:(j + 1) * TKV],
                                    wk[:, kc, oc * 128:(oc + 1) * 128],
                                    klnv[:, kc, :],
                                    start=(kc == 0), stop=(kc == CH - 1))
                        return inst
                    op("tensor", kproj, reads=[B["wkv"], klnB], writes=[psB])
                    op("vector", lambda h, og=og, ps=ps: h.tensor_copy(
                           kp[:, 2 * og:2 * og + 2, :], slots(ps, TKV, 2)),
                       reads=[psB], writes=[B["kp"]])
                    yield

                for hg in range(4):
                    hs = (2 * hg, 2 * hg + 1)
                    def fmk_m(h, hs=hs):
                        inst = None
                        for j, hh in enumerate(hs):
                            h.matmul(pkM0[:, j * TKV:(j + 1) * TKV],
                                     fmw["fmk1"][:, :], kp[:, hh, :],
                                     start=True, stop=True)
                        for j, hh in enumerate(hs):
                            inst = h.matmul(pkM1[:, j * TKV:(j + 1) * TKV],
                                            fmw["fmk2"][:, :], kp[:, hh, :],
                                            start=True, stop=True)
                        return inst
                    op("tensor", fmk_m, reads=[B["fmw"], B["kp"]],
                       writes=[B["pkM0"], B["pkM1"]])
                    op("vector", lambda h, hs=hs: h.tensor_scalar(
                           out=ksq[:, hs[0]:hs[0] + 2, :],
                           in0=slots(pkM1, TKV, 2),
                           scalar1=vecs["fbk2"][:, 0:1], scalar2=None,
                           op0=ALU.add),
                       reads=[B["pkM1"], B["vecs"]], writes=[B["ksq"]])
                    op("vector", lambda h, hs=hs: h.scalar_tensor_tensor(
                           out=kf[:, hs[0]:hs[0] + 2, :],
                           in0=slots(pkM0, TKV, 2),
                           scalar=vecs["fbk1"][:, 0:1],
                           in1=ksq[:, hs[0]:hs[0] + 2, :],
                           op0=ALU.add, op1=ALU.mult),
                       reads=[B["pkM0"], B["ksq"], B["vecs"]], writes=[B["kf"]])
                    if hg % 2 == 1:
                        yield

                for tb in range(NTBK):
                    def ktr(h, tb=tb):
                        inst = None
                        for hh in range(NH):
                            inst = h.transpose(
                                pkT[:, hh * 128:(hh + 1) * 128],
                                kf[:, hh, tb * 128:(tb + 1) * 128],
                                identb[:, :])
                        return inst
                    op("tensor", ktr, reads=[B["kf"], B["identb"]],
                       writes=[B["pkT"]])
                    op("scalar", lambda h, tb=tb: h.copy(
                           kt_b[:, tb, :, :],
                           pkT.ap().rearrange("p (h d) -> p h d", h=NH)),
                       reads=[B["pkT"]], writes=[B["kt_b"]])
                    if tb % 2 == 1:
                        yield

                for tb in range(NTBK):
                    def vproj(h, tb=tb, klnv=klnv):
                        inst = None
                        for kc in range(CH):
                            h.matmul(pkV[:, 0:512],
                                     klnv[:, kc, tb * 128:(tb + 1) * 128],
                                     wv[:, kc, 0:512],
                                     start=(kc == 0), stop=(kc == CH - 1))
                        for kc in range(CH):
                            inst = h.matmul(pkV[:, 512:1024],
                                            klnv[:, kc, tb * 128:(tb + 1) * 128],
                                            wv[:, kc, 512:1024],
                                            start=(kc == 0), stop=(kc == CH - 1))
                        return inst
                    op("tensor", vproj, reads=[B["wkv"], klnB],
                       writes=[B["pkV"]])
                    op("scalar", lambda h, tb=tb: h.copy(
                           vt_b[:, tb, :, :],
                           pkV.ap().rearrange("p (h d) -> p h d", h=NH)),
                       reads=[B["pkV"]], writes=[B["vt_b"]])
                    yield

                if V2 or (t <= SNAP):
                    def kvmm(h):
                        inst = None
                        for hh in range(NH):
                            for tb in range(NTBK):
                                inst = h.matmul(
                                    pkM1[:, hh * 128:(hh + 1) * 128],
                                    kt_b[:, tb, hh, :], vt_b[:, tb, hh, :],
                                    start=(tb == 0), stop=(tb == NTBK - 1))
                        return inst
                    op("tensor", kvmm, reads=[B["kt_b"], B["vt_b"]],
                       writes=[B["pkM1"]])
                    pview = pkM1.ap().rearrange("p (h d) -> p h d", h=NH)
                    if t == 0:
                        op("vector", lambda h, pview=pview: h.tensor_copy(
                               Sf[:, :, :], pview),
                           reads=[B["pkM1"]], writes=[B["Sf"]])
                    else:
                        op("vector", lambda h, pview=pview: h.tensor_add(
                               Sf[:, :, :], Sf[:, :, :], pview),
                           reads=[B["pkM1"], B["Sf"]], writes=[B["Sf"]])
                    yield

                if x0 >= MOFF:
                    o0 = x0 - MOFF
                    op("gpsimd", lambda h, o0=o0, klnv=klnv: h.dma_start(
                           out=s_ln1.ap()[:, :, o0:o0 + TKV].transpose([1, 0, 2]),
                           in_=klnv[:, :, :]),
                       reads=[klnB], writes=[B["s_ln1"]], sem="K0", dma=True)
                    op("gpsimd", lambda h, o0=o0: h.dma_start(
                           out=s_kf.ap()[:, :, o0:o0 + TKV].transpose([1, 0, 2]),
                           in_=kf[:, :, :]),
                       reads=[B["kf"]], writes=[B["s_kf"]], sem="K1", dma=True)
                    tb0 = o0 // 128
                    op("gpsimd", lambda h, tb0=tb0: h.dma_start(
                           out=s_kt.ap()[tb0:tb0 + NTBK].transpose([1, 0, 2, 3]),
                           in_=kt_b[:, :, :, :]),
                       reads=[B["kt_b"]], writes=[B["s_kt"]], sem="K2", dma=True)
                    op("gpsimd", lambda h, tb0=tb0: h.dma_start(
                           out=s_v.ap()[tb0:tb0 + NTBK].transpose([1, 0, 2, 3]),
                           in_=vt_b[:, :, :, :]),
                       reads=[B["vt_b"]], writes=[B["s_v"]], sem="K3", dma=True)

                yield

            def kv_drain(g):
                for _ in g:
                    pass

            w2stagger = [("wq", "W2a"), ("wo", "W2b"), ("wr", "W2c"),
                         ("wk2", "W2d"), ("wv2", "W2e")]
            w2i = 0

            def w2load():
                nonlocal w2i
                nm, sm = w2stagger[w2i]
                w2i += 1
                src = w_in[nm].ap().rearrange("(k p) d -> p k d", p=128)
                op("gpsimd", lambda h, src=src: h.dma_start(
                       out=wmain[nm][:, :, :], in_=src),
                   writes=[B["wmain"]], sem=sm, dma=True,
                   extra=[sch.last["MX"]])

            for t in range(NKV):
                if t >= 1 and w2i < len(w2stagger):
                    w2load()
                a = gen_kv_front(t)
                f = gen_kv_back(t - 1) if t > 0 else None
                while True:
                    da = next(a, "end") if a is not None else "end"
                    df = next(f, "end") if f is not None else "end"
                    if da == "end" and df == "end":
                        break
            while w2i < len(w2stagger):
                w2load()
            kv_drain(gen_kv_back(NKV - 1))

            op("gpsimd", lambda h: h.nop(),
               extra=[sch.last["K0"], sch.last["K1"], sch.last["K2"],
                      sch.last["K3"], sch.last["DBG"], sch.last["DBG2"]])

            with nc.Block("kv") as blk:
                sch.flush(blk)

        with contextlib.ExitStack() as ms:
            MT = ms.enter_context
            mx = MT(nc.sbuf_tensor("mx", [128, CH, 2, TM], BF16))
            cs = MT(nc.sbuf_tensor("cs", [128, CH, 1], F32))
            csrep = MT(nc.sbuf_tensor("csrep", [128, CH, 128], BF16))
            x1 = MT(nc.sbuf_tensor("x1", [128, CH, 2, TM], BF16))
            ln1m = MT(nc.sbuf_tensor("ln1m", [128, CH, TM], BF16))
            kfm = MT(nc.sbuf_tensor("kfm", [128, NH, TM], BF16))
            ktm = MT(nc.sbuf_tensor("ktm", [128, TM // 128, NH, DK], BF16))
            vtm = MT(nc.sbuf_tensor("vtm", [128, TM // 128, NH, DK], BF16))
            qb = MT(nc.sbuf_tensor("qb", [128, NH, TM], BF16))
            wob = MT(nc.sbuf_tensor("wob", [128, NH, TM], BF16))
            qp = wob
            ab0 = MT(nc.sbuf_tensor("ab0", [128, NH, TM], BF16))
            ab1 = MT(nc.sbuf_tensor("ab1", [128, NH, 128], BF16))
            sqa = MT(nc.sbuf_tensor("sqa", [128, CH, TM], BF16))
            opr = MT(nc.sbuf_tensor("opr", [128, NH, TM], BF16))
            fmt2 = MT(nc.sbuf_tensor("fmt2", [128, NH, TM], BF16))
            atscr = MT(nc.sbuf_tensor("atscr", [128, CH, TM], BF16))
            amb = MT(nc.sbuf_tensor("amb", [128, TM], BF16))
            avb = MT(nc.sbuf_tensor("avb", [128, TM], F32))
            aab = MT(nc.sbuf_tensor("aab", [128, TM], BF16))
            h2 = MT(nc.sbuf_tensor("h2b", [128, CH, TM], BF16))
            h2p = MT(nc.sbuf_tensor("h2p", [128, CH, TM], BF16))
            xrb = MT(nc.sbuf_tensor("xrb", [128, NH, TM], BF16))
            kact = xrb
            xk = MT(nc.sbuf_tensor("xk", [128, NH, TM], BF16))
            sqf = MT(nc.sbuf_tensor("sqf", [128, CH, TM], BF16))
            tscr = MT(nc.sbuf_tensor("tscrm", [128, CH, TM], BF16))
            rbuf = MT(nc.sbuf_tensor("rbuf", [128, CH, TM], BF16))
            out_sb = MT(nc.sbuf_tensor("out_sb", [128, 2, HID], F32))
            fstage = MT(nc.sbuf_tensor("fstage", [128, CH, TM], F32))
            mbuf = MT(nc.sbuf_tensor("mbuf", [128, TM], BF16))
            vbuf = MT(nc.sbuf_tensor("vbuf", [128, TM], F32))
            abuf = MT(nc.sbuf_tensor("abuf", [128, TM], BF16))
            p0 = MT(nc.psum_tensor("p0", [128, 1024], F32))
            p1 = MT(nc.psum_tensor("p1", [128, 1024], F32))
            p2 = MT(nc.psum_tensor("p2", [128, 1024], F32))
            p3 = MT(nc.psum_tensor("p3", [128, 1024], F32))
            for nm in ("mx0", "mx1", "cs", "csrep", "ln1m", "kfm", "ktm",
                       "vtm",
                       "x10", "x11", "qb", "wob", "ab0", "ab1", "sqa0", "sqa1",
                       "opr", "fmt2", "atscr", "amb", "avb", "aab", "h2",
                       "h2p", "xrb", "xk", "sqf", "tscr", "rbuf", "out_sb0",
                       "out_sb1", "fstage0", "fstage1", "mbuf", "vbuf",
                       "abuf"):
                buf(nm)
            B["qp"] = B["wob"]
            B["kact"] = B["xrb"]

            op("vector", lambda h: h.reduce_sum(
                   cs[:, :, 0:1], wmain["wo"][:, :, :],
                   axis=mybir.AxisListType.X),
               reads=[B["wmain"]], writes=[B["cs"]])
            op("vector", lambda h: h.tensor_copy(
                   csrep[:, :, :], cs[:, :, 0:1].broadcast_to([128, CH, 128])),
               reads=[B["cs"]], writes=[B["csrep"]])
            for nm in ("p0", "p1", "p2", "p3"):
                buf(nm, excl=True)

            def gen_attn(i, mini=False):
                n = 128 if mini else TM
                par = 0 if mini else i % 2
                t0 = 0 if mini else i * TM
                x1v = x1[:, :, par, 0:n]
                x1B = B["x10"] if par == 0 else B["x11"]

                if mini:
                    mxv = mx[:, :, 0, 0:128]
                    ln1v = h2p[:, :, 0:128]
                    ln1B = B["h2p"]
                    src = x_h.ap().rearrange("(k p) t -> p k t", p=128)[
                        :, :, HALF - 128:HALF]
                    op("gpsimd", lambda h, src=src, mxv=mxv: h.dma_start(
                           out=mxv, in_=src),
                       writes=[B["mx0"]], sem="MX", dma=True)
                    op("vector", lambda h, mxv=mxv: h.tensor_copy(
                           opr[:, :, 0:128], mxv),
                       reads=[B["mx0"]], writes=[B["opr"]])
                    op("scalar", lambda h: h.activation(
                           sqa[:, :, 0:128], opr[:, :, 0:128], AF.Square),
                       reads=[B["opr"]], writes=[B["sqa0"], B["sqa1"]])
                    ln_stats_ops([opr[:, kc, 0:128] for kc in range(CH)],
                                 [sqa[:, kc, 0:128] for kc in range(CH)],
                                 B["opr"], [B["sqa0"], B["sqa1"]],
                                 p0[:, 0:128], B["p0"], p0[:, 128:256], B["p0"],
                                 amb[:, 0:128], avb[:, 0:128], aab[:, 0:128],
                                 B["amb"], B["avb"], B["aab"])
                    op("vector", lambda h, mxv=mxv: h.tensor_sub(
                           atscr[:, :, 0:128], mxv,
                           amb[:, None, 0:128].broadcast_to([128, CH, 128])),
                       reads=[B["mx0"], B["amb"]], writes=[B["atscr"]])
                    op("vector", lambda h, ln1v=ln1v: h.tensor_mul(
                           ln1v, atscr[:, :, 0:128],
                           aab[:, None, 0:128].broadcast_to([128, CH, 128])),
                       reads=[B["atscr"], B["aab"]], writes=[B["h2p"]])
                else:
                    mxv = mx[:, :, par, :]
                    mxB = B["mx0"] if par == 0 else B["mx1"]
                    ln1v = ln1m
                    ln1B = B["ln1m"]
                    src = x_h.ap().rearrange("(k p) t -> p k t", p=128)[
                        :, :, MOFF + t0:MOFF + t0 + TM]
                    op("gpsimd", lambda h, src=src, par=par: h.dma_start(
                           out=mx[:, :, par, :], in_=src),
                       writes=[mxB], sem="MX" if par == 0 else "MX1", dma=True)
                    op("sync", lambda h, t0=t0: h.dma_start(
                           out=ln1m[:, :, :],
                           in_=s_ln1.ap()[:, :, t0:t0 + TM].transpose([1, 0, 2])),
                       reads=[B["s_ln1"]], writes=[B["ln1m"]],
                       sem="L0", dma=True)
                    tb0 = t0 // 128
                    op("sync", lambda h, t0=t0: h.dma_start(
                           out=kfm[:, :, :],
                           in_=s_kf.ap()[:, :, t0:t0 + TM].transpose([1, 0, 2])),
                       reads=[B["s_kf"]], writes=[B["kfm"]], sem="LS0", dma=True)
                    op("sync", lambda h, tb0=tb0: h.dma_start(
                           out=ktm[:, :, :, :],
                           in_=s_kt.ap()[tb0:tb0 + 2].transpose([1, 0, 2, 3])),
                       reads=[B["s_kt"]], writes=[B["ktm"]], sem="LS1", dma=True)
                    op("sync", lambda h, tb0=tb0: h.dma_start(
                           out=vtm[:, :, :, :],
                           in_=s_v.ap()[tb0:tb0 + 2].transpose([1, 0, 2, 3])),
                       reads=[B["s_v"]], writes=[B["vtm"]], sem="LS2", dma=True)

                yield

                for half, ps, psB in ((0, p0, B["p0"]), (1, p1, B["p1"])):
                    def qproj(h, half=half, ps=ps, ln1v=ln1v, n=n):
                        inst = None
                        for j, oc in enumerate(range(4 * half, 4 * half + 4)):
                            for kc in range(CH):
                                inst = h.matmul(
                                    ps[:, j * n:(j + 1) * n],
                                    wmain["wq"][:, kc, oc * 128:(oc + 1) * 128],
                                    ln1v[:, kc, :],
                                    start=(kc == 0), stop=(kc == CH - 1))
                        return inst
                    op("tensor", qproj, reads=[B["wmain"], ln1B], writes=[psB])
                    op("vector", lambda h, half=half, ps=ps, n=n: h.tensor_copy(
                           qp[:, 4 * half:4 * half + 4, 0:n], slots(ps, n, 4)),
                       reads=[psB], writes=[B["qp"]])
                    yield

                for hg in range(2):
                    hs = list(range(4 * hg, 4 * hg + 4))
                    def fmq_m(h, hs=hs, n=n):
                        inst = None
                        for j, hh in enumerate(hs):
                            h.matmul(p0[:, j * n:(j + 1) * n], fmw["fmq1"][:, :],
                                     qp[:, hh, 0:n], start=True, stop=True)
                        for j, hh in enumerate(hs):
                            inst = h.matmul(p1[:, j * n:(j + 1) * n],
                                            fmw["fmq2"][:, :], qp[:, hh, 0:n],
                                            start=True, stop=True)
                        return inst
                    op("tensor", fmq_m, reads=[B["fmw"], B["qp"]],
                       writes=[B["p0"], B["p1"]])
                    op("vector", lambda h, hs=hs, n=n: h.tensor_scalar(
                           out=fmt2[:, hs[0]:hs[0] + 4, 0:n],
                           in0=slots(p1, n, 4), scalar1=SCALE,
                           scalar2=vecs["fbq2s"][:, 0:1],
                           op0=ALU.mult, op1=ALU.add),
                       reads=[B["p1"], B["vecs"]], writes=[B["fmt2"]])
                    op("vector", lambda h, hs=hs, n=n: h.scalar_tensor_tensor(
                           out=qb[:, hs[0]:hs[0] + 4, 0:n], in0=slots(p0, n, 4),
                           scalar=vecs["fbq1"][:, 0:1],
                           in1=fmt2[:, hs[0]:hs[0] + 4, 0:n],
                           op0=ALU.add, op1=ALU.mult),
                       reads=[B["p0"], B["fmt2"], B["vecs"]], writes=[B["qb"]])
                    yield

                if not mini:
                    for hg, ps, psB in ((0, p0, B["p0"]), (1, p1, B["p1"])):
                        hs = list(range(4 * hg, 4 * hg + 4))
                        def amm(h, hs=hs, ps=ps):
                            inst = None
                            for j, hh in enumerate(hs):
                                inst = h.matmul(ps[:, j * TM:(j + 1) * TM],
                                                kfm[:, hh, 0:128],
                                                qb[:, hh, :],
                                                start=True, stop=True)
                            return inst
                        op("tensor", amm, reads=[B["kfm"], B["qb"]],
                           writes=[psB])
                        tm0 = trimask[:, None, :].broadcast_to([128, 4, TM])
                        op("vector", lambda h, hs=hs, ps=ps, tm0=tm0:
                           h.tensor_mul(ab0[:, hs[0]:hs[0] + 4, :],
                                        slots(ps, TM, 4), tm0),
                           reads=[psB, B["trimask"]], writes=[B["ab0"]])
                        yield

                    def amm1(h):
                        inst = None
                        for hh in range(NH):
                            inst = h.matmul(p2[:, hh * 128:(hh + 1) * 128],
                                            kfm[:, hh, 128:256],
                                            qb[:, hh, 128:256],
                                            start=True, stop=True)
                        return inst
                    op("tensor", amm1, reads=[B["kfm"], B["qb"]],
                       writes=[B["p2"]])
                    tm1 = trimask[:, None, 0:128].broadcast_to([128, NH, 128])
                    op("vector", lambda h, tm1=tm1: h.tensor_mul(
                           ab1[:, :, :], slots(p2, 128, NH), tm1),
                       reads=[B["p2"], B["trimask"]], writes=[B["ab1"]])
                    yield

                for hg, ps, psB in ((0, p0, B["p0"]), (1, p1, B["p1"])):
                    hs = list(range(4 * hg, 4 * hg + 4))
                    def omm(h, hs=hs, ps=ps, n=n, mini=mini):
                        inst = None
                        for j, hh in enumerate(hs):
                            inst = h.matmul(ps[:, j * n:(j + 1) * n],
                                            Sb[:, hh, :], qb[:, hh, 0:n],
                                            start=True, stop=mini)
                            if not mini:
                                h.matmul(ps[:, j * n:(j + 1) * n],
                                         vtm[:, 0, hh, :], ab0[:, hh, :],
                                         start=False, stop=False)
                                inst = h.matmul(
                                    ps[:, j * n + 128:j * n + 256],
                                    vtm[:, 1, hh, :], ab1[:, hh, :],
                                    start=False, stop=True)
                        return inst
                    rd = [B["Sb"], B["qb"]] + ([] if mini else
                                               [B["vtm"], B["ab0"], B["ab1"]])
                    op("tensor", omm, reads=rd, writes=[psB])
                    yield

                op("scalar", lambda h, n=n: h.activation(
                       sqa[:, 0:4, 0:n], slots(p0, n, 4), AF.Square),
                   reads=[B["p0"]], writes=[B["sqa0"]])
                op("scalar", lambda h, n=n: h.activation(
                       sqa[:, 4:8, 0:n], slots(p1, n, 4), AF.Square),
                   reads=[B["p1"]], writes=[B["sqa1"]])
                tsf = atscr.ap().rearrange("p k t -> p (k t)")
                for rr in range(2):
                    def rmsmm(h, rr=rr, n=n):
                        inst = None
                        for j in range(4):
                            inst = h.matmul(p2[:, j * n:(j + 1) * n],
                                            onesb[:, :],
                                            sqa[:, 4 * rr + j, 0:n],
                                            start=True, stop=True)
                        return inst
                    op("tensor", rmsmm,
                       reads=[B["sqa0"] if rr == 0 else B["sqa1"],
                              B["onesb"]],
                       writes=[B["p2"]])
                    dst = tsf[:, rr * 4 * n:(rr + 1) * 4 * n].rearrange(
                        "p (g t) -> p g t", t=n)
                    op("scalar", lambda h, dst=dst, n=n: h.activation(
                           dst, slots(p2, n, 4), AF.Ln, bias=c_eps,
                           scale=c_i128),
                       reads=[B["p2"], B["consts"]], writes=[B["atscr"]])
                    yield
                tsv = tsf[:, 0:NH * n].rearrange("p (g t) -> p g t", t=n)
                op("scalar", lambda h, tsv=tsv, n=n: h.activation(
                       fmt2[:, :, 0:n], tsv, AF.Exp, scale=c_nh),
                   reads=[B["atscr"], B["consts"]], writes=[B["fmt2"]])
                for hg, ps, psB in ((0, p0, B["p0"]), (1, p1, B["p1"])):
                    op("vector", lambda h, ps=ps, hg=hg, n=n:
                       h.scalar_tensor_tensor(
                           out=opr[:, 4 * hg:4 * hg + 4, 0:n],
                           in0=slots(ps, n, 4), scalar=vecs["rmsw"][:, 0:1],
                           in1=fmt2[:, 4 * hg:4 * hg + 4, 0:n],
                           op0=ALU.mult, op1=ALU.mult),
                       reads=[psB, B["fmt2"], B["vecs"]], writes=[B["opr"]])
                yield

                def csmm(h, n=n):
                    inst = None
                    for vc in range(CH):
                        inst = h.matmul(p2[:, 0:n], csrep[:, vc, :],
                                        opr[:, vc, 0:n],
                                        start=(vc == 0), stop=(vc == CH - 1))
                    return inst
                op("tensor", csmm, reads=[B["csrep"], B["opr"]],
                   writes=[B["p2"]])
                op("vector", lambda h, n=n: h.tensor_scalar_mul(
                       amb[:, 0:n], p2[:, 0:n], 1.0 / HID),
                   reads=[B["p2"]], writes=[B["amb"]])

                for og, ps, psB in ((0, p0, B["p0"]), (1, p1, B["p1"])):
                    def womm(h, og=og, ps=ps, n=n):
                        inst = None
                        for j, oc in enumerate(range(4 * og, 4 * og + 4)):
                            for vc in range(CH):
                                inst = h.matmul(
                                    ps[:, j * n:(j + 1) * n],
                                    wmain["wo"][:, vc, oc * 128:(oc + 1) * 128],
                                    opr[:, vc, 0:n],
                                    start=(vc == 0), stop=(vc == CH - 1))
                        return inst
                    op("tensor", womm, reads=[B["wmain"], B["opr"]],
                       writes=[psB])
                    op("scalar", lambda h, og=og, ps=ps, n=n: h.activation(
                           sqa[:, 4 * og:4 * og + 4, 0:n], slots(ps, n, 4),
                           AF.Square),
                       reads=[psB], writes=[B["sqa0" if og == 0 else "sqa1"]])
                    yield

                def sqmm_a(h, n=n):
                    inst = None
                    for kc in range(4):
                        inst = h.matmul(p2[:, n:2 * n], onesb[:, :],
                                        sqa[:, kc, 0:n],
                                        start=(kc == 0), stop=False)
                    return inst
                op("tensor", sqmm_a, reads=[B["sqa0"], B["onesb"]],
                   writes=[B["p2"]])
                def sqmm_b(h, n=n):
                    inst = None
                    for kc in range(4, CH):
                        inst = h.matmul(p2[:, n:2 * n], onesb[:, :],
                                        sqa[:, kc, 0:n],
                                        start=False, stop=(kc == CH - 1))
                    return inst
                op("tensor", sqmm_b, reads=[B["sqa1"], B["onesb"]],
                   writes=[B["p2"]])
                op("vector", lambda h, n=n: h.tensor_mul(
                       avb[:, 0:n], amb[:, 0:n], amb[:, 0:n]),
                   reads=[B["amb"]], writes=[B["avb"]])
                op("vector", lambda h, n=n: h.scalar_tensor_tensor(
                       out=avb[:, 0:n], in0=p2[:, n:2 * n], scalar=1.0 / HID,
                       in1=avb[:, 0:n], op0=ALU.mult, op1=ALU.subtract),
                   reads=[B["p2"], B["avb"]], writes=[B["avb"]])
                op("scalar", lambda h, n=n: h.activation(
                       avb[:, 0:n], avb[:, 0:n], AF.Ln, bias=c_eps),
                   reads=[B["avb"], B["consts"]], writes=[B["avb"]])
                op("scalar", lambda h, n=n: h.activation(
                       aab[:, 0:n], avb[:, 0:n], AF.Exp, scale=c_nh),
                   reads=[B["avb"], B["consts"]], writes=[B["aab"]])
                yield
                if not mini:
                    def kvmm(h):
                        inst = None
                        for hh in range(NH):
                            for tb in range(2):
                                inst = h.matmul(p2[:, hh * 128:(hh + 1) * 128],
                                                ktm[:, tb, hh, :],
                                                vtm[:, tb, hh, :],
                                                start=(tb == 0), stop=(tb == 1))
                        return inst
                    op("tensor", kvmm, reads=[B["ktm"], B["vtm"]],
                       writes=[B["p2"]])
                    op("vector", lambda h: h.tensor_add(
                           Sf[:, :, :], Sf[:, :, :],
                           p2.ap().rearrange("p (h d) -> p h d", h=NH)),
                       reads=[B["p2"], B["Sf"]], writes=[B["Sf"]])
                    op("scalar", lambda h: h.copy(Sb[:, :, :], Sf[:, :, :]),
                       reads=[B["Sf"]], writes=[B["Sb"]])
                    yield

                mb4 = amb[:, None, 0:n].broadcast_to([128, 4, n])
                ab8 = aab[:, None, 0:n].broadcast_to([128, CH, n])
                for og, ps, psB in ((0, p0, B["p0"]), (1, p1, B["p1"])):
                    op("vector", lambda h, ps=ps, og=og, mb4=mb4, n=n:
                       h.tensor_sub(atscr[:, 4 * og:4 * og + 4, 0:n],
                                    slots(ps, n, 4), mb4),
                       reads=[psB, B["amb"]], writes=[B["atscr"]])
                op("vector", lambda h, ab8=ab8, n=n: h.tensor_mul(
                       atscr[:, :, 0:n], atscr[:, :, 0:n], ab8),
                   reads=[B["atscr"], B["aab"]], writes=[B["atscr"]])
                op("vector", lambda h, mxv=mxv, x1v=x1v, n=n: h.tensor_add(
                       x1v, mxv, atscr[:, :, 0:n]),
                   reads=[B["mx0"] if mini else mxB, B["atscr"]],
                   writes=[x1B])
                yield

                if mini:
                    op("vector", lambda h: h.tensor_copy(opr[:, :, 0:128],
                                                         x1[:, :, 0, 0:128]),
                       reads=[x1B], writes=[B["opr"]])
                    op("scalar", lambda h: h.activation(
                           sqa[:, :, 0:128], opr[:, :, 0:128], AF.Square),
                       reads=[B["opr"]], writes=[B["sqa0"], B["sqa1"]])
                    ln_stats_ops([opr[:, kc, 0:128] for kc in range(CH)],
                                 [sqa[:, kc, 0:128] for kc in range(CH)],
                                 B["opr"], [B["sqa0"], B["sqa1"]],
                                 p2[:, 0:128], B["p2"], p2[:, 128:256],
                                 B["p2"],
                                 amb[:, 0:128], avb[:, 0:128], aab[:, 0:128],
                                 B["amb"], B["avb"], B["aab"])
                    op("vector", lambda h: h.tensor_sub(
                           atscr[:, :, 0:128], x1[:, :, 0, 0:128],
                           amb[:, None, 0:128].broadcast_to([128, CH, 128])),
                       reads=[x1B, B["amb"]], writes=[B["atscr"]])
                    op("vector", lambda h: h.tensor_mul(
                           h2[:, :, 0:128], atscr[:, :, 0:128],
                           aab[:, None, 0:128].broadcast_to([128, CH, 128])),
                       reads=[B["atscr"], B["aab"]], writes=[B["h2"]])
                    op("vector", lambda h: h.tensor_copy(h2c[:, :, 0:1],
                                                         h2[:, :, 127:128]),
                       reads=[B["h2"]], writes=[B["h2c"]])
                yield

            def gen_ffn(i):
                par = i % 2
                t0 = i * TM
                x1v = x1[:, :, par, :]
                x1B = B["x10"] if par == 0 else B["x11"]

                mxv = mx[:, :, par, :]
                mxB2 = B["mx0"] if par == 0 else B["mx1"]
                op("scalar", lambda h, x1v=x1v: h.activation(
                       sqf[:, :, :], x1v, AF.Square),
                   reads=[x1B], writes=[B["sqf"]])
                yield
                ln_stats_ops([mxv[:, kc, :] for kc in range(CH)],
                             [sqf[:, kc, :] for kc in range(CH)],
                             mxB2, B["sqf"],
                             p3[:, 0:TM], B["p3"], p3[:, TM:2 * TM], B["p3"],
                             mbuf[:, :], vbuf[:, :], abuf[:, :],
                             B["mbuf"], B["vbuf"], B["abuf"],
                             split_mm=True)
                yield
                op("vector", lambda h, x1v=x1v: h.tensor_sub(
                       sqf[:, :, :], x1v,
                       mbuf[:, None, :].broadcast_to([128, CH, TM])),
                   reads=[x1B, B["mbuf"]], writes=[B["sqf"]])
                op("vector", lambda h: h.tensor_mul(
                       h2[:, :, :], sqf[:, :, :],
                       abuf[:, None, :].broadcast_to([128, CH, TM])),
                   reads=[B["sqf"], B["abuf"]], writes=[B["h2"]])
                yield

                op("vector", lambda h: h.tensor_copy(h2p[:, :, 1:TM],
                                                     h2[:, :, 0:TM - 1]),
                   reads=[B["h2"]], writes=[B["h2p"]])
                op("gpsimd", lambda h, par=par: h.tensor_copy(
                       h2p[:, :, 0:1], h2c[:, :, par:par + 1]),
                   reads=[B["h2c"]], writes=[B["h2p"]])
                op("gpsimd", lambda h, i=i: h.tensor_copy(
                       h2c[:, :, (i + 1) % 2:(i + 1) % 2 + 1],
                       h2[:, :, TM - 1:TM]),
                   reads=[B["h2"]], writes=[B["h2c"]])
                op("vector", lambda h: h.tensor_sub(sqf[:, :, :], h2[:, :, :],
                                                    h2p[:, :, :]),
                   reads=[B["h2"], B["h2p"]], writes=[B["sqf"]])
                yield
                murb = mur[:, :, None].broadcast_to([128, CH, TM])
                mukb = muk[:, :, None].broadcast_to([128, CH, TM])
                op("vector", lambda h, murb=murb: h.tensor_mul(
                       xrb[:, :, :], sqf[:, :, :], murb),
                   reads=[B["sqf"], B["mur"]], writes=[B["xrb"]])
                op("vector", lambda h: h.tensor_add(xrb[:, :, :], xrb[:, :, :],
                                                    h2p[:, :, :]),
                   reads=[B["h2p"], B["xrb"]], writes=[B["xrb"]])
                op("vector", lambda h, mukb=mukb: h.tensor_mul(
                       xk[:, :, :], sqf[:, :, :], mukb),
                   reads=[B["sqf"], B["muk"]], writes=[B["xk"]])
                op("vector", lambda h: h.tensor_add(xk[:, :, :], xk[:, :, :],
                                                    h2p[:, :, :]),
                   reads=[B["h2p"], B["xk"]], writes=[B["xk"]])
                yield

                def proj_f(wname, src, srcB, og):
                    def mm(h, og=og, src=src):
                        inst = None
                        for j, oc in enumerate(range(4 * og, 4 * og + 4)):
                            for kc in range(CH):
                                inst = h.matmul(
                                    p3[:, j * TM:(j + 1) * TM],
                                    wmain[wname][:, kc,
                                                 oc * 128:(oc + 1) * 128],
                                    src[:, kc, :],
                                    start=(kc == 0), stop=(kc == CH - 1))
                        return inst
                    op("tensor", mm, reads=[B["wmain"], srcB], writes=[B["p3"]])

                for og in range(2):
                    proj_f("wr", xrb, B["xrb"], og)
                    op("scalar", lambda h, og=og: h.activation(
                           tscr[:, 4 * og:4 * og + 4, :], slots(p3, TM, 4),
                           AF.Exp, scale=c_n1),
                       reads=[B["p3"], B["consts"]], writes=[B["tscr"]])
                    yield
                op("scalar", lambda h: h.activation(
                       tscr[:, :, :], tscr[:, :, :], AF.Ln, bias=c_one),
                   reads=[B["tscr"], B["consts"]], writes=[B["tscr"]])
                op("scalar", lambda h: h.activation(
                       rbuf[:, :, :], tscr[:, :, :], AF.Exp, scale=c_n1),
                   reads=[B["tscr"], B["consts"]], writes=[B["rbuf"]])
                yield
                for og, psx, psxB in ((0, p2, B["p2"]), (1, p3, B["p3"])):
                    def wk2mm(h, og=og, psx=psx):
                        inst = None
                        for j, oc in enumerate(range(4 * og, 4 * og + 4)):
                            for kc in range(CH):
                                inst = h.matmul(
                                    psx[:, j * TM:(j + 1) * TM],
                                    wmain["wk2"][:, kc,
                                                 oc * 128:(oc + 1) * 128],
                                    xk[:, kc, :],
                                    start=(kc == 0), stop=(kc == CH - 1))
                        return inst
                    op("tensor", wk2mm, reads=[B["wmain"], B["xk"]],
                       writes=[psxB])
                    op("scalar", lambda h, og=og, psx=psx: h.activation(
                           sqf[:, 4 * og:4 * og + 4, :], slots(psx, TM, 4),
                           AF.Relu),
                       reads=[psxB], writes=[B["sqf"]])
                    yield
                op("scalar", lambda h: h.activation(kact[:, :, :],
                                                    sqf[:, :, :], AF.Square),
                   reads=[B["sqf"]], writes=[B["kact"]])
                yield
                for og in range(2):
                    proj_f("wv2", kact, B["kact"], og)
                    op("vector", lambda h, og=og: h.tensor_mul(
                           tscr[:, 4 * og:4 * og + 4, :],
                           rbuf[:, 4 * og:4 * og + 4, :], slots(p3, TM, 4)),
                       reads=[B["p3"], B["rbuf"]], writes=[B["tscr"]])
                    yield
                op("scalar", lambda h: h.activation(sqf[:, :, :],
                                                    tscr[:, :, :], AF.Square),
                   reads=[B["tscr"]], writes=[B["sqf"]])
                yield
                ln_stats_ops([tscr[:, kc, :] for kc in range(CH)],
                             [sqf[:, kc, :] for kc in range(CH)],
                             B["tscr"], B["sqf"],
                             p3[:, 0:TM], B["p3"], p3[:, TM:2 * TM], B["p3"],
                             mbuf[:, :], vbuf[:, :], abuf[:, :],
                             B["mbuf"], B["vbuf"], B["abuf"],
                             split_mm=True)
                yield
                op("vector", lambda h: h.tensor_sub(
                       rbuf[:, :, :], tscr[:, :, :],
                       mbuf[:, None, :].broadcast_to([128, CH, TM])),
                   reads=[B["tscr"], B["mbuf"]], writes=[B["rbuf"]])
                op("vector", lambda h: h.tensor_mul(
                       tscr[:, :, :], rbuf[:, :, :],
                       abuf[:, None, :].broadcast_to([128, CH, TM])),
                   reads=[B["rbuf"], B["abuf"]], writes=[B["tscr"]])
                op("vector", lambda h, x1v=x1v: h.tensor_add(
                       fstage[:, :, 0:128], x1v[:, :, 0:128],
                       tscr[:, :, 0:128]),
                   reads=[B["tscr"], x1B], writes=[B["fstage0"]])
                op("vector", lambda h, x1v=x1v: h.tensor_add(
                       fstage[:, :, 128:TM], x1v[:, :, 128:TM],
                       tscr[:, :, 128:TM]),
                   reads=[B["tscr"], x1B], writes=[B["fstage1"]])
                yield

                for tb in range(2):
                    def trp(h, tb=tb):
                        inst = None
                        for ck in range(CH):
                            inst = h.transpose(
                                p3[:, ck * 128:(ck + 1) * 128],
                                fstage[:, ck, tb * 128:(tb + 1) * 128],
                                ident32[:, :])
                        return inst
                    op("tensor", trp,
                       reads=[B["fstage0"] if tb == 0 else B["fstage1"],
                              B["ident32"]],
                       writes=[B["p3"]])
                    op("vector", lambda h, tb=tb: h.tensor_copy(
                           out_sb[:, tb, :], p3[:, :]),
                       reads=[B["p3"]], writes=[B["out_sb0" if tb == 0
                                                 else "out_sb1"]])
                    r0 = t0 + tb * 128
                    op("gpsimd", lambda h, r0=r0, tb=tb: h.dma_start(
                           out=y_h.ap()[r0:r0 + 128, :], in_=out_sb[:, tb, :]),
                       reads=[B["out_sb0" if tb == 0 else "out_sb1"]],
                       sem="ST0" if tb == 0 else "ST1", dma=True)
                    yield

            def drain(g):
                for _ in g:
                    pass

            def zipper(a, f):
                while True:
                    da = next(a, "end") if a is not None else "end"
                    df = next(f, "end") if f is not None else "end"
                    if da == "end" and df == "end":
                        break

            op("vector", lambda h: h.tensor_copy(Sb[:, :, :], Sf[:, :, :]),
               reads=[B["Sf"]], writes=[B["Sb"]])
            if do_mini:
                drain(gen_attn(0, mini=True))

            if V2:
                pay = fstage.ap().rearrange("p k t -> p (k t)")[:, 0:1032]
                payB = [B["fstage0"], B["fstage1"]]
                h2cf = h2c.ap().rearrange("p k t -> p (k t)")
                op("vector", lambda h: h.tensor_scalar(
                       out=fstage.ap().rearrange("p k t -> p (k t)")[:, 0:1024],
                       in0=Sf.ap().rearrange("p h d -> p (h d)"),
                       scalar1=sflag[:, 0:1], scalar2=None, op0=ALU.mult),
                   reads=[B["Sf"], B["flags"]], writes=payB)
                op("vector", lambda h: h.tensor_scalar(
                       out=fstage.ap().rearrange("p k t -> p (k t)")[:, 1024:1032],
                       in0=h2cf[:, 0:16:2],
                       scalar1=sflag[:, 0:1], scalar2=None, op0=ALU.mult),
                   reads=[B["h2c"], B["flags"]], writes=payB)
                ev1 = op("gpsimd", lambda h: h.dma_start(out=cc_in.ap(),
                                                         in_=pay),
                         reads=payB, sem="CCI", dma=True)
                ccB = buf("cc")
                op("gpsimd", lambda h: h.collective_compute(
                       "AllReduce", ALU.add,
                       replica_groups=[[0, 1], [2, 3], [4, 5], [6, 7]],
                       ins=[cc_in.ap()], outs=[cc_out.ap()]),
                   extra=[ev1], writes=[ccB], sem="CC")
                op("sync", lambda h: h.dma_start(out=pay, in_=cc_out.ap()),
                   reads=[ccB], writes=payB, sem="CC2", dma=True)
                op("vector", lambda h: h.tensor_scalar(
                       out=Sf.ap().rearrange("p h d -> p (h d)"),
                       in0=fstage.ap().rearrange("p k t -> p (k t)")[:, 0:1024],
                       scalar1=uflag[:, 0:1], scalar2=None, op0=ALU.mult),
                   reads=payB + [B["flags"]], writes=[B["Sf"]])
                op("vector", lambda h: h.tensor_scalar(
                       out=h2cf[:, 0:16:2],
                       in0=fstage.ap().rearrange("p k t -> p (k t)")[:, 1024:1032],
                       scalar1=uflag[:, 0:1], scalar2=None, op0=ALU.mult),
                   reads=payB + [B["flags"]], writes=[B["h2c"]])
                op("vector", lambda h: h.tensor_copy(Sb[:, :, :], Sf[:, :, :]),
                   reads=[B["Sf"]], writes=[B["Sb"]])

            NMR = NM if n_main is None else min(n_main, NM)
            for i in range(NMR):
                zipper(gen_attn(i), gen_ffn(i - 1) if i > 0 else None)
            if NMR > 0:
                drain(gen_ffn(NMR - 1))

            if dbg:
                for src, dst in (("s_ln1", "d_ln1"), ("s_kf", "d_kf"),
                                 ("s_kt", "d_kt"), ("s_v", "d_v")):
                    op("sync", lambda h, src=src, dst=dst: h.dma_start(
                           out=d_outs[dst].ap(),
                           in_=(s_ln1 if src == "s_ln1" else
                                s_kf if src == "s_kf" else
                                s_kt if src == "s_kt" else s_v).ap()),
                       reads=[B[src]], sem="DBG", dma=True)
                op("sync", lambda h: h.dma_start(
                       out=d_outs["d_sf"].ap(),
                       in_=Sf.ap().rearrange("p h d -> p (h d)")),
                   reads=[B["Sf"]], sem="DBG", dma=True)
                op("gpsimd", lambda h: h.dma_start(
                       out=d_outs["d_h2c"].ap(),
                       in_=h2c.ap().rearrange("p k t -> p (k t)")),
                   reads=[B["h2c"]], sem="DBG2", dma=True)
            op("sync", lambda h: h.nop(),
               extra=[sch.last["ST0"], sch.last["ST1"], sch.last["DBG"],
                      sch.last["DBG2"]])
            with nc.Block("main") as blk:
                sch.flush(blk)

    return nc


_V2 = True


def kernel(**inp):
    x = np.asarray(inp["L0_attn_hidden_states"])
    Bn, T, C = x.shape
    HALF = T // 2
    nc = build_program(HALF=HALF, V2=_V2)

    wmap = {
        "wq": "L0_attn_Wq", "wk": "L0_attn_Wk", "wv": "L0_attn_Wv",
        "wo": "L0_attn_Wo", "wr": "L0_ffn_W_r", "wk2": "L0_ffn_W_k",
        "wv2": "L0_ffn_W_v", "fmq1": "L0_attn_fmq_w1",
        "fmq2": "L0_attn_fmq_w2", "fmk1": "L0_attn_fmk_w1",
        "fmk2": "L0_attn_fmk_w2",
    }
    base = {}
    for k, src in wmap.items():
        base[k] = np.ascontiguousarray(np.asarray(inp[src]).T.astype(np.float32))
    for k, src in (("fbq1", "L0_attn_fmq_b1"), ("fbq2", "L0_attn_fmq_b2"),
                   ("fbk1", "L0_attn_fmk_b1"), ("fbk2", "L0_attn_fmk_b2"),
                   ("rmsw", "L0_attn_rms_w")):
        base[k] = np.ascontiguousarray(np.asarray(inp[src], dtype=np.float32))
    base["mur"] = np.ascontiguousarray(
        np.asarray(inp["L0_ffn_mu_r"], dtype=np.float32).reshape(CH, 128).T)
    base["muk"] = np.ascontiguousarray(
        np.asarray(inp["L0_ffn_mu_k"], dtype=np.float32).reshape(CH, 128).T)
    base["ident"] = np.eye(128, dtype=np.float32)
    base["trimask"] = np.ascontiguousarray(np.concatenate(
        [np.triu(np.ones((128, 128), np.float32)),
         np.ones((128, 128), np.float32)], axis=1))

    in_maps = []
    for c in range(8):
        b, hf = c // 2, c % 2
        own = x[b, hf * HALF:(hf + 1) * HALF]
        m = dict(base)
        if _V2:
            m["xkv"] = np.ascontiguousarray(own.T.astype(np.float32))
            m["flags"] = np.ascontiguousarray(np.broadcast_to(
                np.array([1.0, 0.0] if hf == 0 else [0.0, 1.0], np.float32),
                (128, 2)))
        else:
            pre = np.zeros((HALF, C), np.float32) if hf == 0 else x[b, :HALF]
            m["xkv"] = np.ascontiguousarray(
                np.concatenate([pre, own], axis=0).T.astype(np.float32))
        in_maps.append(m)

    res = run_bass_kernel_spmd(nc, in_maps, core_ids=list(range(8)))
    out = np.empty((Bn, T, C), np.float32)
    for c in range(8):
        b, hf = c // 2, c % 2
        out[b, hf * HALF:(hf + 1) * HALF] = res.results[c]["y"]
    return out


# version 22
# speedup vs baseline: 1.0626x; 1.0626x over previous
import contextlib
import numpy as np
import concourse.bass as bass
import concourse.mybir as mybir
from concourse.bass_utils import run_bass_kernel_spmd

F32 = mybir.dt.float32
BF16 = mybir.dt.bfloat16
AF = mybir.ActivationFunctionType
ALU = mybir.AluOpType

HID = 1024
NH = 8
DK = 128
CH = 8
SCALE = float(DK) ** -0.5
EPS = 1e-5

ENGS = ("tensor", "vector", "scalar", "gpsimd", "sync")
ENG_SEM = {"tensor": "T", "vector": "V", "scalar": "A", "gpsimd": "G", "sync": "S"}


class Buf:
    __slots__ = ("w", "readers", "track", "excl")

    def __init__(self, track=True, excl=False):
        self.w = None
        self.readers = []
        self.track = track
        self.excl = excl


class Op:
    __slots__ = ("oid", "eng", "fn", "deps", "semn", "inc", "est", "cnt",
                 "issue")

    def __init__(self, oid, eng, fn, deps, semn, inc, est, issue):
        self.oid = oid
        self.eng = eng
        self.fn = fn
        self.deps = deps
        self.semn = semn
        self.inc = inc
        self.est = est
        self.issue = issue
        self.cnt = None


HOP_NS = 300.0

_DEF_EST = {"tensor": 1500.0, "vector": 900.0, "scalar": 1100.0,
            "gpsimd": 1500.0, "sync": 600.0}


class Sched:
    def __init__(self, sems):
        self.sems = sems
        self.cnt = {n: 0 for n in sems}
        self.ops = []
        self.last = {n: None for n in sems}
        self._oid = 0

    def op(self, eng, fn, reads=(), writes=(), sem=None, dma=False, extra=(),
           est=None):
        semn = sem or ENG_SEM[eng]
        inc = 16 if dma else 1
        if est is None:
            est = 3000.0 if dma else _DEF_EST[eng]
        issue = 300.0 if dma else est
        reads = list(reads)
        writes = list(writes)
        moved = [b for b in reads if b.excl]
        if moved:
            reads = [b for b in reads if not b.excl]
            writes = writes + [b for b in moved if b not in writes]
        deps = {}
        for b in reads:
            if b.w is not None:
                deps[b.w.oid] = b.w
        for b in writes:
            if b.w is not None:
                deps[b.w.oid] = b.w
            for r in b.readers:
                deps[r.oid] = r
        for e in extra:
            if e is not None:
                deps[e.oid] = e
        if dma and self.last[semn] is not None:
            lo = self.last[semn]
            deps[lo.oid] = lo
        o = Op(self._oid, eng, fn, list(deps.values()), semn, inc, est, issue)
        self._oid += 1
        self.cnt[semn] += inc
        self.last[semn] = o
        self.ops.append(o)
        for b in reads:
            if b.track:
                b.readers.append(o)
        for b in writes:
            b.w = o
            b.readers = []
        return o

    def pin(self, semn, bufs):
        lo = self.last[semn]
        for b in bufs:
            if b.w is not None and b.w.semn == semn:
                b.w = lo
            b.readers = [lo if (r.semn == semn) else r for r in b.readers]
        return lo

    def flush(self, block):
        ops = self.ops
        self.ops = []
        if not ops:
            return
        idx = {o.oid: o for o in ops}
        succ = {o.oid: [] for o in ops}
        ndep = {o.oid: 0 for o in ops}
        for o in ops:
            for d in o.deps:
                if d.oid in idx:
                    succ[d.oid].append(o)
                    ndep[o.oid] += 1
        cp = {}
        for o in reversed(ops):
            m = 0.0
            for s2 in succ[o.oid]:
                m = max(m, cp[s2.oid])
            cp[o.oid] = m + o.est
        import heapq
        engs = sorted({o.eng for o in ops})
        readyq = {e: [] for e in engs}
        rtime = {}
        done_t = {}
        eng_t = {e: 0.0 for e in engs}
        seq = {e: [] for e in engs}

        def push_ready(o):
            rt = 0.0
            for d in o.deps:
                if d.oid in done_t:
                    rt = max(rt, done_t[d.oid] +
                             (HOP_NS if d.eng != o.eng else 0.0))
            rtime[o.oid] = rt
            heapq.heappush(readyq[o.eng], (-cp[o.oid], o.oid))

        for o in ops:
            if ndep[o.oid] == 0:
                push_ready(o)
        n_left = len(ops)
        while n_left:
            best = None
            for e in engs:
                if readyq[e]:
                    _, oid = readyq[e][0]
                    o = idx[oid]
                    st = max(eng_t[e], rtime[oid])
                    key = (st, -cp[oid])
                    if best is None or key < best[0]:
                        best = (key, e)
            _, e = best
            _, oid = heapq.heappop(readyq[e])
            o = idx[oid]
            st = max(eng_t[e], rtime[oid])
            done_t[oid] = st + o.est
            eng_t[e] = st + o.issue
            seq[e].append(o)
            n_left -= 1
            for s2 in succ[oid]:
                ndep[s2.oid] -= 1
                if ndep[s2.oid] == 0:
                    push_ready(s2)

        cnt = {n: 0 for n in self.sems}
        for n in self.sems:
            cnt[n] = self._base.get(n, 0) if hasattr(self, "_base") else 0
        if not hasattr(self, "_base"):
            self._base = {}
        for e in engs:
            for o in seq[e]:
                cnt[o.semn] += o.inc
                o.cnt = cnt[o.semn]
        self._base = dict(cnt)

        for e in engs:
            if not seq[e]:
                continue

            def body(h, e=e, olist=seq[e]):
                waited = {}
                for o in olist:
                    ws = {}
                    for d in o.deps:
                        assert d.cnt is not None, "dep scheduled later!"
                        if ws.get(d.semn, 0) < d.cnt:
                            ws[d.semn] = d.cnt
                    for wsem in sorted(ws):
                        wval = ws[wsem]
                        if waited.get(wsem, 0) < wval:
                            h.wait_ge(self.sems[wsem], wval)
                            waited[wsem] = wval
                    inst = o.fn(h)
                    if inst is None:
                        inst = h.nop()
                    inst.then_inc(self.sems[o.semn], o.inc)

            getattr(block, e)(body)


def slots(p, n, k):
    return p.ap().rearrange("p (g t) -> p g t", t=n)[:, 0:k, :]


def build_program(HALF=4096, V2=False, n_main=None, do_mini=True, mini_stage=99, dbg=False):
    TKV = 512
    TM = 256
    NKV = (2 * HALF) // TKV if not V2 else HALF // TKV
    SNAP = (HALF // TKV - 1) if not V2 else (NKV - 1)
    NM = HALF // TM
    NTB = HALF // 128
    NTBK = TKV // 128
    XTOT = 2 * HALF if not V2 else HALF
    MOFF = XTOT - HALF

    nc = bass.Bass("TRN2", target_bir_lowering=False)

    x_h = nc.dram_tensor("xkv", [HID, XTOT], F32, kind="ExternalInput")
    w_in = {}
    for nm in ("wq", "wk", "wv", "wo", "wr", "wk2", "wv2"):
        w_in[nm] = nc.dram_tensor(nm, [HID, HID], F32, kind="ExternalInput")
    for nm in ("fmq1", "fmq2", "fmk1", "fmk2"):
        w_in[nm] = nc.dram_tensor(nm, [DK, DK], F32, kind="ExternalInput")
    vec_in = {}
    for nm in ("fbq1", "fbq2", "fbk1", "fbk2", "rmsw"):
        vec_in[nm] = nc.dram_tensor(nm, [DK], F32, kind="ExternalInput")
    mur_h = nc.dram_tensor("mur", [128, CH], F32, kind="ExternalInput")
    muk_h = nc.dram_tensor("muk", [128, CH], F32, kind="ExternalInput")
    ident_h = nc.dram_tensor("ident", [128, 128], F32, kind="ExternalInput")
    trimask_h = nc.dram_tensor("trimask", [128, 256], F32, kind="ExternalInput")
    y_h = nc.dram_tensor("y", [HALF, HID], F32, kind="ExternalOutput")
    if dbg:
        d_outs = {
            "d_ln1": nc.dram_tensor("d_ln1", [CH, 128, HALF], BF16,
                                    kind="ExternalOutput"),
            "d_kf": nc.dram_tensor("d_kf", [NH, 128, HALF], BF16,
                                   kind="ExternalOutput"),
            "d_kt": nc.dram_tensor("d_kt", [HALF // 128, 128, NH, DK], BF16,
                                   kind="ExternalOutput"),
            "d_v": nc.dram_tensor("d_v", [HALF // 128, 128, NH, DK], BF16,
                                  kind="ExternalOutput"),
            "d_sf": nc.dram_tensor("d_sf", [128, NH * DK], F32,
                                   kind="ExternalOutput"),
            "d_h2c": nc.dram_tensor("d_h2c", [128, CH * 2], F32,
                                    kind="ExternalOutput"),
            "d_m": nc.dram_tensor("d_m", [128, 512], F32,
                                  kind="ExternalOutput"),
            "d_a": nc.dram_tensor("d_a", [128, 512], F32,
                                  kind="ExternalOutput"),
            "d_xb": nc.dram_tensor("d_xb", [128, CH * 512], F32,
                                   kind="ExternalOutput"),
            "d_sq": nc.dram_tensor("d_sq", [128, CH * 512], F32,
                                   kind="ExternalOutput"),
        }
    s_ln1 = nc.dram_tensor("s_ln1", [CH, 128, HALF], BF16)
    s_kf = nc.dram_tensor("s_kf", [NH, 128, HALF], BF16)
    s_kt = nc.dram_tensor("s_kt", [NTB, 128, NH, DK], BF16)
    s_v = nc.dram_tensor("s_v", [NTB, 128, NH, DK], BF16)
    if V2:
        flg_h = nc.dram_tensor("flags", [128, 2], F32, kind="ExternalInput")
        cc_in = nc.dram_tensor("cc_in", [128, 1032], F32)
        cc_out = nc.dram_tensor("cc_out", [128, 1032], F32)

    with contextlib.ExitStack() as stack:
        ST = stack.enter_context
        sems = {}
        for nm in ("T", "V", "A", "G", "S", "SC", "SX", "L0", "L1",
                   "WA", "WK", "WV", "W2a", "W2b", "W2c", "W2d", "W2e",
                   "ST0", "ST1", "DBG", "DBG2", "K0", "K1", "K2", "K3",
                   "CC", "CC2", "CCI", "LS0", "LS1", "LS2", "SX1",
                   "MX", "MX1"):
            sems[nm] = ST(nc.semaphore("sem" + nm))
        sch = Sched(sems)
        op = sch.op

        ident32 = ST(nc.sbuf_tensor("ident32", [128, 128], F32))
        identb = ST(nc.sbuf_tensor("identb", [128, 128], BF16))
        onesb = ST(nc.sbuf_tensor("onesb", [128, 128], BF16))
        trimask = ST(nc.sbuf_tensor("trimask_s", [128, 256], F32))
        consts = ST(nc.sbuf_tensor("consts", [128, 8], F32))
        tmpv = ST(nc.sbuf_tensor("tmpv", [128, 8], F32))
        vecs = {nm: ST(nc.sbuf_tensor(nm + "_s", [128, 1], F32))
                for nm in ("fbq1", "fbq2s", "fbk1", "fbk2", "rmsw")}
        mur = ST(nc.sbuf_tensor("mur_s", [128, CH], BF16))
        muk = ST(nc.sbuf_tensor("muk_s", [128, CH], BF16))
        fmw = {nm: ST(nc.sbuf_tensor(nm + "_s", [128, 128], BF16))
               for nm in ("fmq1", "fmq2", "fmk1", "fmk2")}
        wmain = {nm: ST(nc.sbuf_tensor(nm + "_s", [128, CH, HID], BF16))
                 for nm in ("wq", "wo", "wr", "wk2", "wv2")}
        Sf = ST(nc.sbuf_tensor("Sf", [128, NH, DK], F32))
        Sb = ST(nc.sbuf_tensor("Sb", [128, NH, DK], BF16))
        h2c = ST(nc.sbuf_tensor("h2c", [128, CH, 2], BF16))
        if V2:
            sflag = ST(nc.sbuf_tensor("sflag", [128, 1], F32))
            uflag = ST(nc.sbuf_tensor("uflag", [128, 1], F32))

        B = {}

        def buf(name, track=True, excl=False):
            B[name] = Buf(track, excl)
            return B[name]

        for nm in ("ident32", "identb", "onesb", "trimask", "mur", "muk",
                   "fmw"):
            buf(nm, track=False)
        for nm in ("consts", "vecs", "tmpv", "Sf",
                   "Sb", "h2c", "wmain", "wkv", "s_ln1", "s_kf", "s_kt",
                   "s_v", "flags"):
            buf(nm)

        c_eps = consts[:, 0:1]
        c_nh = consts[:, 1:2]
        c_n1 = consts[:, 2:3]
        c_i128 = consts[:, 3:4]
        c_one = consts[:, 4:5]

        op("sync", lambda h: h.dma_start(out=ident32[:, :], in_=ident_h.ap()),
           writes=[B["ident32"]], sem="SC", dma=True)
        op("sync", lambda h: h.dma_start(out=trimask[:, :], in_=trimask_h.ap()),
           writes=[B["trimask"]], sem="SC", dma=True)
        op("gpsimd", lambda h: h.dma_start(out=mur[:, :], in_=mur_h.ap()),
           writes=[B["mur"]], sem="WA", dma=True)
        op("gpsimd", lambda h: h.dma_start(out=muk[:, :], in_=muk_h.ap()),
           writes=[B["muk"]], sem="WA", dma=True)
        if V2:
            op("sync", lambda h: h.dma_start(
                   out=sflag[:, 0:1],
                   in_=flg_h.ap()[:, 0:1],
                   allow_slow_non_contiguous=True),
               writes=[B["flags"]], sem="SC", dma=True)
            op("sync", lambda h: h.dma_start(
                   out=uflag[:, 0:1],
                   in_=flg_h.ap()[:, 1:2],
                   allow_slow_non_contiguous=True),
               writes=[B["flags"]], sem="SC", dma=True)
        op("gpsimd", lambda h: h.dma_start(out=identb[:, :], in_=ident_h.ap()),
           writes=[B["identb"]], sem="WA", dma=True)
        op("gpsimd", lambda h: h.memset(onesb[:, :], 1.0), writes=[B["onesb"]])
        op("gpsimd", lambda h: h.memset(consts[:, 0:1], EPS), writes=[B["consts"]])
        op("gpsimd", lambda h: h.memset(consts[:, 1:2], -0.5), writes=[B["consts"]])
        op("gpsimd", lambda h: h.memset(consts[:, 2:3], -1.0), writes=[B["consts"]])
        op("gpsimd", lambda h: h.memset(consts[:, 3:4], 1.0 / DK),
           writes=[B["consts"]])
        op("gpsimd", lambda h: h.memset(consts[:, 4:5], 1.0),
           writes=[B["consts"]])
        for i, nm in enumerate(("fbq1", "fbq2", "fbk1", "fbk2", "rmsw")):
            op("sync",
               lambda h, i=i, nm=nm: h.dma_start(out=tmpv[:, i:i + 1],
                                                 in_=vec_in[nm].ap()[:, None]),
               writes=[B["tmpv"]], sem="SC", dma=True)
        sch.pin("SC", [B["ident32"], B["trimask"], B["mur"], B["muk"],
                       B["tmpv"]] + ([B["flags"]] if V2 else []))
        op("vector", lambda h: h.tensor_scalar_mul(vecs["fbq2s"][:, 0:1],
                                                   tmpv[:, 1:2], SCALE),
           reads=[B["tmpv"]], writes=[B["vecs"]])
        for i, nm in ((0, "fbq1"), (2, "fbk1"), (3, "fbk2"), (4, "rmsw")):
            op("vector", lambda h, i=i, nm=nm: h.tensor_copy(vecs[nm][:, 0:1],
                                                             tmpv[:, i:i + 1]),
               reads=[B["tmpv"]], writes=[B["vecs"]])
        for nm in ("fmq1", "fmq2", "fmk1", "fmk2"):
            op("gpsimd", lambda h, nm=nm: h.dma_start(out=fmw[nm][:, :],
                                                      in_=w_in[nm].ap()),
               writes=[B["fmw"]], sem="WA", dma=True)

        def load_weight(nm, dst, semn, bname):
            src = w_in[nm].ap().rearrange("(k p) d -> p k d", p=128)
            op("gpsimd", lambda h: h.dma_start(out=dst[:, :, :], in_=src),
               writes=[B[bname]], sem=semn, dma=True)

        def ln_stats_ops(xb_sl, sq_sl, xbB, sqB, ps_s, psB_s, ps_q, psB_q,
                         mb, vb, ab, mB, vB, aB, seq_sumsq=False):
            sqBs = sqB if isinstance(sqB, list) else [sqB]
            if seq_sumsq:
                def mms1(h):
                    inst = None
                    for kc in range(CH):
                        inst = h.matmul(ps_s, onesb[:, :], xb_sl[kc],
                                        start=(kc == 0), stop=(kc == CH - 1))
                    return inst
                op("tensor", mms1, reads=[xbB, B["onesb"]], writes=[psB_s])
                op("vector", lambda h: h.tensor_scalar_mul(mb, ps_s, 1.0 / HID),
                   reads=[psB_s], writes=[mB])
                def mms2(h):
                    inst = None
                    for kc in range(CH):
                        inst = h.matmul(ps_q, onesb[:, :], sq_sl[kc],
                                        start=(kc == 0), stop=(kc == CH - 1))
                    return inst
                op("tensor", mms2, reads=sqBs + [B["onesb"]], writes=[psB_q])
            else:
                def mms(h):
                    for kc in range(CH):
                        h.matmul(ps_s, onesb[:, :], xb_sl[kc],
                                 start=(kc == 0), stop=(kc == CH - 1))
                    inst = None
                    for kc in range(CH):
                        inst = h.matmul(ps_q, onesb[:, :], sq_sl[kc],
                                        start=(kc == 0), stop=(kc == CH - 1))
                    return inst
                wr = [psB_s] if psB_s is psB_q else [psB_s, psB_q]
                op("tensor", mms, reads=[xbB] + sqBs + [B["onesb"]], writes=wr)
                op("vector", lambda h: h.tensor_scalar_mul(mb, ps_s, 1.0 / HID),
                   reads=[psB_s], writes=[mB])
            op("vector", lambda h: h.tensor_mul(vb, mb, mb),
               reads=[mB], writes=[vB])
            op("vector", lambda h: h.scalar_tensor_tensor(
                   out=vb, in0=ps_q, scalar=1.0 / HID, in1=vb,
                   op0=ALU.mult, op1=ALU.subtract),
               reads=[psB_q, vB], writes=[vB])
            op("scalar", lambda h: h.activation(vb, vb, AF.Ln, bias=c_eps),
               reads=[vB, B["consts"]], writes=[vB])
            op("scalar", lambda h: h.activation(ab, vb, AF.Exp, scale=c_nh),
               reads=[vB, B["consts"]], writes=[aB])

        with contextlib.ExitStack() as kvs:
            KT = kvs.enter_context
            wk = KT(nc.sbuf_tensor("wk_s", [128, CH, HID], BF16))
            wv = KT(nc.sbuf_tensor("wv_s", [128, CH, HID], BF16))
            kxb = KT(nc.sbuf_tensor("kxb", [128, CH, TKV], BF16))
            kf = KT(nc.sbuf_tensor("kf", [128, NH, TKV], BF16))
            ksq = KT(nc.sbuf_tensor("ksq", [128, CH, TKV], BF16))
            kln = KT(nc.sbuf_tensor("kln", [128, 2, CH, TKV], BF16))
            kp = KT(nc.sbuf_tensor("kp", [128, NH, TKV], BF16))
            kt_b = KT(nc.sbuf_tensor("kt_b", [128, NTBK, NH, DK], BF16))
            vt_b = KT(nc.sbuf_tensor("vt_b", [128, NTBK, NH, DK], BF16))
            mbufk = KT(nc.sbuf_tensor("mbufk", [128, TKV], BF16))
            vbufk = KT(nc.sbuf_tensor("vbufk", [128, TKV], F32))
            abufk = KT(nc.sbuf_tensor("abufk", [128, TKV], BF16))
            pkM0 = KT(nc.psum_tensor("pkM0", [128, 1024], F32))
            pkM1 = KT(nc.psum_tensor("pkM1", [128, 1024], F32))
            pkV = KT(nc.psum_tensor("pkV", [128, 1024], F32))
            pkT = KT(nc.psum_tensor("pkT", [128, 1024], BF16))
            pkS = KT(nc.psum_tensor("pkS", [128, 512], F32))
            for nm in ("kxb", "kf", "ksq", "kln0a", "kln0b", "kln1a",
                       "kln1b", "kp", "kt_b",
                       "vt_b", "mbufk", "vbufk", "abufk"):
                buf(nm)
            for nm in ("pkM0", "pkM1", "pkV", "pkT", "pkS"):
                buf(nm, excl=True)

            load_weight("wk", wk, "WK", "wkv")
            load_weight("wv", wv, "WV", "wkv")

            def gen_kv_front(t):
                x0 = t * TKV
                par = t % 2
                klnv = kln[:, par]
                klnBa = B["kln0a"] if par == 0 else B["kln1a"]
                klnBb = B["kln0b"] if par == 0 else B["kln1b"]
                src = x_h.ap().rearrange("(k p) t -> p k t", p=128)[
                    :, :, x0:x0 + TKV]
                op("gpsimd", lambda h, src=src: h.dma_start(
                       out=kxb[:, :, :], in_=src),
                   writes=[B["kxb"]], sem="MX", dma=True)
                op("scalar", lambda h: h.activation(ksq[:, :, :], kxb[:, :, :],
                                                    AF.Square),
                   reads=[B["kxb"]], writes=[B["ksq"]])
                yield
                ln_stats_ops([kxb[:, kc, :] for kc in range(CH)],
                             [ksq[:, kc, :] for kc in range(CH)],
                             B["kxb"], B["ksq"],
                             pkS[:, 0:TKV], B["pkS"], pkS[:, 0:TKV], B["pkS"],
                             mbufk[:, :], vbufk[:, :], abufk[:, :],
                             B["mbufk"], B["vbufk"], B["abufk"],
                             seq_sumsq=True)
                yield
                mbv = mbufk[:, None, :].broadcast_to([128, 4, TKV])
                abv = abufk[:, None, :].broadcast_to([128, 4, TKV])
                for cg, hB in ((0, klnBa), (1, klnBb)):
                    op("vector", lambda h, mbv=mbv, klnv=klnv, cg=cg:
                       h.tensor_sub(klnv[:, 4 * cg:4 * cg + 4, :],
                                    kxb[:, 4 * cg:4 * cg + 4, :], mbv),
                       reads=[B["kxb"], B["mbufk"]], writes=[hB])
                    op("vector", lambda h, abv=abv, klnv=klnv, cg=cg:
                       h.tensor_mul(klnv[:, 4 * cg:4 * cg + 4, :],
                                    klnv[:, 4 * cg:4 * cg + 4, :], abv),
                       reads=[B["abufk"], hB], writes=[hB])
                yield

            def gen_kv_back(t):
                x0 = t * TKV
                par = t % 2
                klnv = kln[:, par]
                klnBa = B["kln0a"] if par == 0 else B["kln1a"]
                klnBb = B["kln0b"] if par == 0 else B["kln1b"]

                for og in range(4):
                    ps, psB = ((pkM1, B["pkM1"]), (pkM0, B["pkM0"]))[og % 2]
                    def kproj_a(h, og=og, ps=ps, klnv=klnv):
                        inst = None
                        for j, oc in enumerate((2 * og, 2 * og + 1)):
                            for kc in range(4):
                                inst = h.matmul(
                                    ps[:, j * TKV:(j + 1) * TKV],
                                    wk[:, kc, oc * 128:(oc + 1) * 128],
                                    klnv[:, kc, :],
                                    start=(kc == 0), stop=False)
                        return inst
                    op("tensor", kproj_a, reads=[B["wkv"], klnBa],
                       writes=[psB])
                    def kproj_b(h, og=og, ps=ps, klnv=klnv):
                        inst = None
                        for j, oc in enumerate((2 * og, 2 * og + 1)):
                            for kc in range(4, CH):
                                inst = h.matmul(
                                    ps[:, j * TKV:(j + 1) * TKV],
                                    wk[:, kc, oc * 128:(oc + 1) * 128],
                                    klnv[:, kc, :],
                                    start=False, stop=(kc == CH - 1))
                        return inst
                    op("tensor", kproj_b, reads=[B["wkv"], klnBb],
                       writes=[psB])
                    op("vector", lambda h, og=og, ps=ps: h.tensor_copy(
                           kp[:, 2 * og:2 * og + 2, :], slots(ps, TKV, 2)),
                       reads=[psB], writes=[B["kp"]])
                    yield

                for hg in range(4):
                    hs = (2 * hg, 2 * hg + 1)
                    def fmk_m(h, hs=hs):
                        inst = None
                        for j, hh in enumerate(hs):
                            h.matmul(pkM0[:, j * TKV:(j + 1) * TKV],
                                     fmw["fmk1"][:, :], kp[:, hh, :],
                                     start=True, stop=True)
                        for j, hh in enumerate(hs):
                            inst = h.matmul(pkM1[:, j * TKV:(j + 1) * TKV],
                                            fmw["fmk2"][:, :], kp[:, hh, :],
                                            start=True, stop=True)
                        return inst
                    op("tensor", fmk_m, reads=[B["fmw"], B["kp"]],
                       writes=[B["pkM0"], B["pkM1"]])
                    op("vector", lambda h, hs=hs: h.tensor_scalar(
                           out=ksq[:, hs[0]:hs[0] + 2, :],
                           in0=slots(pkM1, TKV, 2),
                           scalar1=vecs["fbk2"][:, 0:1], scalar2=None,
                           op0=ALU.add),
                       reads=[B["pkM1"], B["vecs"]], writes=[B["ksq"]])
                    op("vector", lambda h, hs=hs: h.scalar_tensor_tensor(
                           out=kf[:, hs[0]:hs[0] + 2, :],
                           in0=slots(pkM0, TKV, 2),
                           scalar=vecs["fbk1"][:, 0:1],
                           in1=ksq[:, hs[0]:hs[0] + 2, :],
                           op0=ALU.add, op1=ALU.mult),
                       reads=[B["pkM0"], B["ksq"], B["vecs"]], writes=[B["kf"]])
                    if hg % 2 == 1:
                        yield

                for tb in range(NTBK):
                    def ktr(h, tb=tb):
                        inst = None
                        for hh in range(NH):
                            inst = h.transpose(
                                pkT[:, hh * 128:(hh + 1) * 128],
                                kf[:, hh, tb * 128:(tb + 1) * 128],
                                identb[:, :])
                        return inst
                    op("tensor", ktr, reads=[B["kf"], B["identb"]],
                       writes=[B["pkT"]])
                    op("scalar", lambda h, tb=tb: h.copy(
                           kt_b[:, tb, :, :],
                           pkT.ap().rearrange("p (h d) -> p h d", h=NH)),
                       reads=[B["pkT"]], writes=[B["kt_b"]])
                    if tb % 2 == 1:
                        yield

                for tb in range(NTBK):
                    def vproj(h, tb=tb, klnv=klnv):
                        inst = None
                        for kc in range(CH):
                            h.matmul(pkV[:, 0:512],
                                     klnv[:, kc, tb * 128:(tb + 1) * 128],
                                     wv[:, kc, 0:512],
                                     start=(kc == 0), stop=(kc == CH - 1))
                        for kc in range(CH):
                            inst = h.matmul(pkV[:, 512:1024],
                                            klnv[:, kc, tb * 128:(tb + 1) * 128],
                                            wv[:, kc, 512:1024],
                                            start=(kc == 0), stop=(kc == CH - 1))
                        return inst
                    op("tensor", vproj, reads=[B["wkv"], klnBa, klnBb],
                       writes=[B["pkV"]])
                    op("scalar", lambda h, tb=tb: h.copy(
                           vt_b[:, tb, :, :],
                           pkV.ap().rearrange("p (h d) -> p h d", h=NH)),
                       reads=[B["pkV"]], writes=[B["vt_b"]])
                    yield

                if V2 or (t <= SNAP):
                    def kvmm(h):
                        inst = None
                        for hh in range(NH):
                            for tb in range(NTBK):
                                inst = h.matmul(
                                    pkM1[:, hh * 128:(hh + 1) * 128],
                                    kt_b[:, tb, hh, :], vt_b[:, tb, hh, :],
                                    start=(tb == 0), stop=(tb == NTBK - 1))
                        return inst
                    op("tensor", kvmm, reads=[B["kt_b"], B["vt_b"]],
                       writes=[B["pkM1"]])
                    pview = pkM1.ap().rearrange("p (h d) -> p h d", h=NH)
                    if t == 0:
                        op("vector", lambda h, pview=pview: h.tensor_copy(
                               Sf[:, :, :], pview),
                           reads=[B["pkM1"]], writes=[B["Sf"]])
                    else:
                        op("vector", lambda h, pview=pview: h.tensor_add(
                               Sf[:, :, :], Sf[:, :, :], pview),
                           reads=[B["pkM1"], B["Sf"]], writes=[B["Sf"]])
                    yield

                if x0 >= MOFF:
                    o0 = x0 - MOFF
                    op("gpsimd", lambda h, o0=o0, klnv=klnv: h.dma_start(
                           out=s_ln1.ap()[:, :, o0:o0 + TKV].transpose([1, 0, 2]),
                           in_=klnv[:, :, :]),
                       reads=[klnBa, klnBb], writes=[B["s_ln1"]], sem="K0",
                       dma=True)
                    op("gpsimd", lambda h, o0=o0: h.dma_start(
                           out=s_kf.ap()[:, :, o0:o0 + TKV].transpose([1, 0, 2]),
                           in_=kf[:, :, :]),
                       reads=[B["kf"]], writes=[B["s_kf"]], sem="K1", dma=True)
                    tb0 = o0 // 128
                    op("gpsimd", lambda h, tb0=tb0: h.dma_start(
                           out=s_kt.ap()[tb0:tb0 + NTBK].transpose([1, 0, 2, 3]),
                           in_=kt_b[:, :, :, :]),
                       reads=[B["kt_b"]], writes=[B["s_kt"]], sem="K2", dma=True)
                    op("gpsimd", lambda h, tb0=tb0: h.dma_start(
                           out=s_v.ap()[tb0:tb0 + NTBK].transpose([1, 0, 2, 3]),
                           in_=vt_b[:, :, :, :]),
                       reads=[B["vt_b"]], writes=[B["s_v"]], sem="K3", dma=True)

                yield

            def kv_drain(g):
                for _ in g:
                    pass

            w2stagger = [("wq", "W2a"), ("wo", "W2b"), ("wr", "W2c"),
                         ("wk2", "W2d"), ("wv2", "W2e")]
            w2i = 0

            def w2load():
                nonlocal w2i
                nm, sm = w2stagger[w2i]
                w2i += 1
                src = w_in[nm].ap().rearrange("(k p) d -> p k d", p=128)
                op("gpsimd", lambda h, src=src: h.dma_start(
                       out=wmain[nm][:, :, :], in_=src),
                   writes=[B["wmain"]], sem=sm, dma=True,
                   extra=[sch.last["MX"]])

            for t in range(NKV):
                if t >= 1 and w2i < len(w2stagger):
                    w2load()
                a = gen_kv_front(t)
                f = gen_kv_back(t - 1) if t > 0 else None
                while True:
                    da = next(a, "end") if a is not None else "end"
                    df = next(f, "end") if f is not None else "end"
                    if da == "end" and df == "end":
                        break
            while w2i < len(w2stagger):
                w2load()
            kv_drain(gen_kv_back(NKV - 1))

            op("gpsimd", lambda h: h.nop(),
               extra=[sch.last["K0"], sch.last["K1"], sch.last["K2"],
                      sch.last["K3"], sch.last["DBG"], sch.last["DBG2"]])

            with nc.Block("kv") as blk:
                sch.flush(blk)

        with contextlib.ExitStack() as ms:
            MT = ms.enter_context
            mx = MT(nc.sbuf_tensor("mx", [128, CH, 2, TM], BF16))
            cs = MT(nc.sbuf_tensor("cs", [128, CH, 1], F32))
            csrep = MT(nc.sbuf_tensor("csrep", [128, CH, 128], BF16))
            x1 = MT(nc.sbuf_tensor("x1", [128, CH, 2, TM], BF16))
            ln1m = MT(nc.sbuf_tensor("ln1m", [128, CH, TM], BF16))
            kfm = MT(nc.sbuf_tensor("kfm", [128, NH, TM], BF16))
            ktm = MT(nc.sbuf_tensor("ktm", [128, TM // 128, NH, DK], BF16))
            vtm = MT(nc.sbuf_tensor("vtm", [128, TM // 128, NH, DK], BF16))
            qb = MT(nc.sbuf_tensor("qb", [128, NH, TM], BF16))
            wob = MT(nc.sbuf_tensor("wob", [128, NH, TM], BF16))
            qp = wob
            ab0 = MT(nc.sbuf_tensor("ab0", [128, NH, TM], BF16))
            ab1 = MT(nc.sbuf_tensor("ab1", [128, NH, 128], BF16))
            sqa = MT(nc.sbuf_tensor("sqa", [128, CH, TM], BF16))
            opr = MT(nc.sbuf_tensor("opr", [128, NH, TM], BF16))
            fmt2 = MT(nc.sbuf_tensor("fmt2", [128, NH, TM], BF16))
            atscr = MT(nc.sbuf_tensor("atscr", [128, CH, TM], BF16))
            amb = MT(nc.sbuf_tensor("amb", [128, TM], BF16))
            avb = MT(nc.sbuf_tensor("avb", [128, TM], F32))
            aab = MT(nc.sbuf_tensor("aab", [128, TM], BF16))
            h2 = MT(nc.sbuf_tensor("h2b", [128, CH, TM], BF16))
            h2p = MT(nc.sbuf_tensor("h2p", [128, CH, TM], BF16))
            xrb = MT(nc.sbuf_tensor("xrb", [128, NH, TM], BF16))
            kact = xrb
            xk = MT(nc.sbuf_tensor("xk", [128, NH, TM], BF16))
            sqf = MT(nc.sbuf_tensor("sqf", [128, CH, TM], BF16))
            tscr = MT(nc.sbuf_tensor("tscrm", [128, CH, TM], BF16))
            rbuf = MT(nc.sbuf_tensor("rbuf", [128, CH, TM], BF16))
            out_sb = MT(nc.sbuf_tensor("out_sb", [128, 2, HID], F32))
            fstage = MT(nc.sbuf_tensor("fstage", [128, CH, TM], F32))
            mbuf = MT(nc.sbuf_tensor("mbuf", [128, TM], BF16))
            vbuf = MT(nc.sbuf_tensor("vbuf", [128, TM], F32))
            abuf = MT(nc.sbuf_tensor("abuf", [128, TM], BF16))
            p0 = MT(nc.psum_tensor("p0", [128, 1024], F32))
            p1 = MT(nc.psum_tensor("p1", [128, 1024], F32))
            p2 = MT(nc.psum_tensor("p2", [128, 1024], F32))
            p3 = MT(nc.psum_tensor("p3", [128, 1024], F32))
            for nm in ("mx0", "mx1", "cs", "csrep", "ln1m", "kfm", "ktm",
                       "vtm",
                       "x10", "x11", "qb", "wob", "ab0", "ab1", "sqa0", "sqa1",
                       "opr", "fmt2", "atscr", "amb", "avb", "aab", "h2",
                       "h2p", "xrb", "xk", "sqf", "tscr", "rbuf", "out_sb0",
                       "out_sb1", "fstage0", "fstage1", "mbuf", "vbuf",
                       "abuf"):
                buf(nm)
            B["qp"] = B["wob"]
            B["kact"] = B["xrb"]

            op("vector", lambda h: h.reduce_sum(
                   cs[:, :, 0:1], wmain["wo"][:, :, :],
                   axis=mybir.AxisListType.X),
               reads=[B["wmain"]], writes=[B["cs"]])
            op("vector", lambda h: h.tensor_copy(
                   csrep[:, :, :], cs[:, :, 0:1].broadcast_to([128, CH, 128])),
               reads=[B["cs"]], writes=[B["csrep"]])
            for nm in ("p0", "p1", "p2", "p3"):
                buf(nm, excl=True)

            def gen_attn(i, mini=False):
                n = 128 if mini else TM
                par = 0 if mini else i % 2
                t0 = 0 if mini else i * TM
                x1v = x1[:, :, par, 0:n]
                x1B = B["x10"] if par == 0 else B["x11"]

                if mini:
                    mxv = mx[:, :, 0, 0:128]
                    ln1v = h2p[:, :, 0:128]
                    ln1B = B["h2p"]
                    src = x_h.ap().rearrange("(k p) t -> p k t", p=128)[
                        :, :, HALF - 128:HALF]
                    op("gpsimd", lambda h, src=src, mxv=mxv: h.dma_start(
                           out=mxv, in_=src),
                       writes=[B["mx0"]], sem="MX", dma=True)
                    op("vector", lambda h, mxv=mxv: h.tensor_copy(
                           opr[:, :, 0:128], mxv),
                       reads=[B["mx0"]], writes=[B["opr"]])
                    op("scalar", lambda h: h.activation(
                           sqa[:, :, 0:128], opr[:, :, 0:128], AF.Square),
                       reads=[B["opr"]], writes=[B["sqa0"], B["sqa1"]])
                    ln_stats_ops([opr[:, kc, 0:128] for kc in range(CH)],
                                 [sqa[:, kc, 0:128] for kc in range(CH)],
                                 B["opr"], [B["sqa0"], B["sqa1"]],
                                 p0[:, 0:128], B["p0"], p0[:, 128:256], B["p0"],
                                 amb[:, 0:128], avb[:, 0:128], aab[:, 0:128],
                                 B["amb"], B["avb"], B["aab"])
                    op("vector", lambda h, mxv=mxv: h.tensor_sub(
                           atscr[:, :, 0:128], mxv,
                           amb[:, None, 0:128].broadcast_to([128, CH, 128])),
                       reads=[B["mx0"], B["amb"]], writes=[B["atscr"]])
                    op("vector", lambda h, ln1v=ln1v: h.tensor_mul(
                           ln1v, atscr[:, :, 0:128],
                           aab[:, None, 0:128].broadcast_to([128, CH, 128])),
                       reads=[B["atscr"], B["aab"]], writes=[B["h2p"]])
                else:
                    mxv = mx[:, :, par, :]
                    mxB = B["mx0"] if par == 0 else B["mx1"]
                    ln1v = ln1m
                    ln1B = B["ln1m"]
                    src = x_h.ap().rearrange("(k p) t -> p k t", p=128)[
                        :, :, MOFF + t0:MOFF + t0 + TM]
                    op("gpsimd", lambda h, src=src, par=par: h.dma_start(
                           out=mx[:, :, par, :], in_=src),
                       writes=[mxB], sem="MX" if par == 0 else "MX1", dma=True)
                    op("sync", lambda h, t0=t0: h.dma_start(
                           out=ln1m[:, :, :],
                           in_=s_ln1.ap()[:, :, t0:t0 + TM].transpose([1, 0, 2])),
                       reads=[B["s_ln1"]], writes=[B["ln1m"]],
                       sem="L0", dma=True)
                    tb0 = t0 // 128
                    op("sync", lambda h, t0=t0: h.dma_start(
                           out=kfm[:, :, :],
                           in_=s_kf.ap()[:, :, t0:t0 + TM].transpose([1, 0, 2])),
                       reads=[B["s_kf"]], writes=[B["kfm"]], sem="LS0", dma=True)
                    op("sync", lambda h, tb0=tb0: h.dma_start(
                           out=ktm[:, :, :, :],
                           in_=s_kt.ap()[tb0:tb0 + 2].transpose([1, 0, 2, 3])),
                       reads=[B["s_kt"]], writes=[B["ktm"]], sem="LS1", dma=True)
                    op("sync", lambda h, tb0=tb0: h.dma_start(
                           out=vtm[:, :, :, :],
                           in_=s_v.ap()[tb0:tb0 + 2].transpose([1, 0, 2, 3])),
                       reads=[B["s_v"]], writes=[B["vtm"]], sem="LS2", dma=True)

                yield

                for half, ps, psB in ((0, p0, B["p0"]), (1, p1, B["p1"])):
                    def qproj(h, half=half, ps=ps, ln1v=ln1v, n=n):
                        inst = None
                        for j, oc in enumerate(range(4 * half, 4 * half + 4)):
                            for kc in range(CH):
                                inst = h.matmul(
                                    ps[:, j * n:(j + 1) * n],
                                    wmain["wq"][:, kc, oc * 128:(oc + 1) * 128],
                                    ln1v[:, kc, :],
                                    start=(kc == 0), stop=(kc == CH - 1))
                        return inst
                    op("tensor", qproj, reads=[B["wmain"], ln1B], writes=[psB])
                    op("vector", lambda h, half=half, ps=ps, n=n: h.tensor_copy(
                           qp[:, 4 * half:4 * half + 4, 0:n], slots(ps, n, 4)),
                       reads=[psB], writes=[B["qp"]])
                    yield

                for hg in range(2):
                    hs = list(range(4 * hg, 4 * hg + 4))
                    def fmq_m(h, hs=hs, n=n):
                        inst = None
                        for j, hh in enumerate(hs):
                            h.matmul(p0[:, j * n:(j + 1) * n], fmw["fmq1"][:, :],
                                     qp[:, hh, 0:n], start=True, stop=True)
                        for j, hh in enumerate(hs):
                            inst = h.matmul(p1[:, j * n:(j + 1) * n],
                                            fmw["fmq2"][:, :], qp[:, hh, 0:n],
                                            start=True, stop=True)
                        return inst
                    op("tensor", fmq_m, reads=[B["fmw"], B["qp"]],
                       writes=[B["p0"], B["p1"]])
                    op("vector", lambda h, hs=hs, n=n: h.tensor_scalar(
                           out=fmt2[:, hs[0]:hs[0] + 4, 0:n],
                           in0=slots(p1, n, 4), scalar1=SCALE,
                           scalar2=vecs["fbq2s"][:, 0:1],
                           op0=ALU.mult, op1=ALU.add),
                       reads=[B["p1"], B["vecs"]], writes=[B["fmt2"]])
                    op("vector", lambda h, hs=hs, n=n: h.scalar_tensor_tensor(
                           out=qb[:, hs[0]:hs[0] + 4, 0:n], in0=slots(p0, n, 4),
                           scalar=vecs["fbq1"][:, 0:1],
                           in1=fmt2[:, hs[0]:hs[0] + 4, 0:n],
                           op0=ALU.add, op1=ALU.mult),
                       reads=[B["p0"], B["fmt2"], B["vecs"]], writes=[B["qb"]])
                    yield

                if not mini:
                    for hg, ps, psB in ((0, p0, B["p0"]), (1, p1, B["p1"])):
                        hs = list(range(4 * hg, 4 * hg + 4))
                        def amm(h, hs=hs, ps=ps):
                            inst = None
                            for j, hh in enumerate(hs):
                                inst = h.matmul(ps[:, j * TM:(j + 1) * TM],
                                                kfm[:, hh, 0:128],
                                                qb[:, hh, :],
                                                start=True, stop=True)
                            return inst
                        op("tensor", amm, reads=[B["kfm"], B["qb"]],
                           writes=[psB])
                        tm0 = trimask[:, None, :].broadcast_to([128, 4, TM])
                        op("vector", lambda h, hs=hs, ps=ps, tm0=tm0:
                           h.tensor_mul(ab0[:, hs[0]:hs[0] + 4, :],
                                        slots(ps, TM, 4), tm0),
                           reads=[psB, B["trimask"]], writes=[B["ab0"]])
                        yield

                    def amm1(h):
                        inst = None
                        for hh in range(NH):
                            inst = h.matmul(p2[:, hh * 128:(hh + 1) * 128],
                                            kfm[:, hh, 128:256],
                                            qb[:, hh, 128:256],
                                            start=True, stop=True)
                        return inst
                    op("tensor", amm1, reads=[B["kfm"], B["qb"]],
                       writes=[B["p2"]])
                    tm1 = trimask[:, None, 0:128].broadcast_to([128, NH, 128])
                    op("vector", lambda h, tm1=tm1: h.tensor_mul(
                           ab1[:, :, :], slots(p2, 128, NH), tm1),
                       reads=[B["p2"], B["trimask"]], writes=[B["ab1"]])
                    yield

                for hg, ps, psB in ((0, p0, B["p0"]), (1, p1, B["p1"])):
                    hs = list(range(4 * hg, 4 * hg + 4))
                    def omm(h, hs=hs, ps=ps, n=n, mini=mini):
                        inst = None
                        for j, hh in enumerate(hs):
                            inst = h.matmul(ps[:, j * n:(j + 1) * n],
                                            Sb[:, hh, :], qb[:, hh, 0:n],
                                            start=True, stop=mini)
                            if not mini:
                                h.matmul(ps[:, j * n:(j + 1) * n],
                                         vtm[:, 0, hh, :], ab0[:, hh, :],
                                         start=False, stop=False)
                                inst = h.matmul(
                                    ps[:, j * n + 128:j * n + 256],
                                    vtm[:, 1, hh, :], ab1[:, hh, :],
                                    start=False, stop=True)
                        return inst
                    rd = [B["Sb"], B["qb"]] + ([] if mini else
                                               [B["vtm"], B["ab0"], B["ab1"]])
                    op("tensor", omm, reads=rd, writes=[psB])
                    yield

                op("scalar", lambda h, n=n: h.activation(
                       sqa[:, 0:4, 0:n], slots(p0, n, 4), AF.Square),
                   reads=[B["p0"]], writes=[B["sqa0"]])
                op("scalar", lambda h, n=n: h.activation(
                       sqa[:, 4:8, 0:n], slots(p1, n, 4), AF.Square),
                   reads=[B["p1"]], writes=[B["sqa1"]])
                tsf = atscr.ap().rearrange("p k t -> p (k t)")
                for rr in range(2):
                    def rmsmm(h, rr=rr, n=n):
                        inst = None
                        for j in range(4):
                            inst = h.matmul(p2[:, j * n:(j + 1) * n],
                                            onesb[:, :],
                                            sqa[:, 4 * rr + j, 0:n],
                                            start=True, stop=True)
                        return inst
                    op("tensor", rmsmm,
                       reads=[B["sqa0"] if rr == 0 else B["sqa1"],
                              B["onesb"]],
                       writes=[B["p2"]])
                    dst = tsf[:, rr * 4 * n:(rr + 1) * 4 * n].rearrange(
                        "p (g t) -> p g t", t=n)
                    op("scalar", lambda h, dst=dst, n=n: h.activation(
                           dst, slots(p2, n, 4), AF.Ln, bias=c_eps,
                           scale=c_i128),
                       reads=[B["p2"], B["consts"]], writes=[B["atscr"]])
                    yield
                tsv = tsf[:, 0:NH * n].rearrange("p (g t) -> p g t", t=n)
                op("scalar", lambda h, tsv=tsv, n=n: h.activation(
                       fmt2[:, :, 0:n], tsv, AF.Exp, scale=c_nh),
                   reads=[B["atscr"], B["consts"]], writes=[B["fmt2"]])
                for hg, ps, psB in ((0, p0, B["p0"]), (1, p1, B["p1"])):
                    op("vector", lambda h, ps=ps, hg=hg, n=n:
                       h.scalar_tensor_tensor(
                           out=opr[:, 4 * hg:4 * hg + 4, 0:n],
                           in0=slots(ps, n, 4), scalar=vecs["rmsw"][:, 0:1],
                           in1=fmt2[:, 4 * hg:4 * hg + 4, 0:n],
                           op0=ALU.mult, op1=ALU.mult),
                       reads=[psB, B["fmt2"], B["vecs"]], writes=[B["opr"]])
                yield

                def csmm(h, n=n):
                    inst = None
                    for vc in range(CH):
                        inst = h.matmul(p2[:, 0:n], csrep[:, vc, :],
                                        opr[:, vc, 0:n],
                                        start=(vc == 0), stop=(vc == CH - 1))
                    return inst
                op("tensor", csmm, reads=[B["csrep"], B["opr"]],
                   writes=[B["p2"]])
                op("vector", lambda h, n=n: h.tensor_scalar_mul(
                       amb[:, 0:n], p2[:, 0:n], 1.0 / HID),
                   reads=[B["p2"]], writes=[B["amb"]])

                for og, ps, psB in ((0, p0, B["p0"]), (1, p1, B["p1"])):
                    def womm(h, og=og, ps=ps, n=n):
                        inst = None
                        for j, oc in enumerate(range(4 * og, 4 * og + 4)):
                            for vc in range(CH):
                                inst = h.matmul(
                                    ps[:, j * n:(j + 1) * n],
                                    wmain["wo"][:, vc, oc * 128:(oc + 1) * 128],
                                    opr[:, vc, 0:n],
                                    start=(vc == 0), stop=(vc == CH - 1))
                        return inst
                    op("tensor", womm, reads=[B["wmain"], B["opr"]],
                       writes=[psB])
                    op("scalar", lambda h, og=og, ps=ps, n=n: h.activation(
                           sqa[:, 4 * og:4 * og + 4, 0:n], slots(ps, n, 4),
                           AF.Square),
                       reads=[psB], writes=[B["sqa0" if og == 0 else "sqa1"]])
                    yield

                def sqmm_a(h, n=n):
                    inst = None
                    for kc in range(4):
                        inst = h.matmul(p2[:, n:2 * n], onesb[:, :],
                                        sqa[:, kc, 0:n],
                                        start=(kc == 0), stop=False)
                    return inst
                op("tensor", sqmm_a, reads=[B["sqa0"], B["onesb"]],
                   writes=[B["p2"]])
                def sqmm_b(h, n=n):
                    inst = None
                    for kc in range(4, CH):
                        inst = h.matmul(p2[:, n:2 * n], onesb[:, :],
                                        sqa[:, kc, 0:n],
                                        start=False, stop=(kc == CH - 1))
                    return inst
                op("tensor", sqmm_b, reads=[B["sqa1"], B["onesb"]],
                   writes=[B["p2"]])
                op("vector", lambda h, n=n: h.tensor_mul(
                       avb[:, 0:n], amb[:, 0:n], amb[:, 0:n]),
                   reads=[B["amb"]], writes=[B["avb"]])
                op("vector", lambda h, n=n: h.scalar_tensor_tensor(
                       out=avb[:, 0:n], in0=p2[:, n:2 * n], scalar=1.0 / HID,
                       in1=avb[:, 0:n], op0=ALU.mult, op1=ALU.subtract),
                   reads=[B["p2"], B["avb"]], writes=[B["avb"]])
                op("scalar", lambda h, n=n: h.activation(
                       avb[:, 0:n], avb[:, 0:n], AF.Ln, bias=c_eps),
                   reads=[B["avb"], B["consts"]], writes=[B["avb"]])
                op("scalar", lambda h, n=n: h.activation(
                       aab[:, 0:n], avb[:, 0:n], AF.Exp, scale=c_nh),
                   reads=[B["avb"], B["consts"]], writes=[B["aab"]])
                yield
                if not mini:
                    def kvmm(h):
                        inst = None
                        for hh in range(NH):
                            for tb in range(2):
                                inst = h.matmul(p2[:, hh * 128:(hh + 1) * 128],
                                                ktm[:, tb, hh, :],
                                                vtm[:, tb, hh, :],
                                                start=(tb == 0), stop=(tb == 1))
                        return inst
                    op("tensor", kvmm, reads=[B["ktm"], B["vtm"]],
                       writes=[B["p2"]])
                    op("vector", lambda h: h.tensor_add(
                           Sf[:, :, :], Sf[:, :, :],
                           p2.ap().rearrange("p (h d) -> p h d", h=NH)),
                       reads=[B["p2"], B["Sf"]], writes=[B["Sf"]])
                    op("scalar", lambda h: h.copy(Sb[:, :, :], Sf[:, :, :]),
                       reads=[B["Sf"]], writes=[B["Sb"]])
                    yield

                mb4 = amb[:, None, 0:n].broadcast_to([128, 4, n])
                ab8 = aab[:, None, 0:n].broadcast_to([128, CH, n])
                for og, ps, psB in ((0, p0, B["p0"]), (1, p1, B["p1"])):
                    op("vector", lambda h, ps=ps, og=og, mb4=mb4, n=n:
                       h.tensor_sub(atscr[:, 4 * og:4 * og + 4, 0:n],
                                    slots(ps, n, 4), mb4),
                       reads=[psB, B["amb"]], writes=[B["atscr"]])
                op("vector", lambda h, ab8=ab8, n=n: h.tensor_mul(
                       atscr[:, :, 0:n], atscr[:, :, 0:n], ab8),
                   reads=[B["atscr"], B["aab"]], writes=[B["atscr"]])
                op("vector", lambda h, mxv=mxv, x1v=x1v, n=n: h.tensor_add(
                       x1v, mxv, atscr[:, :, 0:n]),
                   reads=[B["mx0"] if mini else mxB, B["atscr"]],
                   writes=[x1B])
                yield

                if mini:
                    op("vector", lambda h: h.tensor_copy(opr[:, :, 0:128],
                                                         x1[:, :, 0, 0:128]),
                       reads=[x1B], writes=[B["opr"]])
                    op("scalar", lambda h: h.activation(
                           sqa[:, :, 0:128], opr[:, :, 0:128], AF.Square),
                       reads=[B["opr"]], writes=[B["sqa0"], B["sqa1"]])
                    ln_stats_ops([opr[:, kc, 0:128] for kc in range(CH)],
                                 [sqa[:, kc, 0:128] for kc in range(CH)],
                                 B["opr"], [B["sqa0"], B["sqa1"]],
                                 p2[:, 0:128], B["p2"], p2[:, 128:256],
                                 B["p2"],
                                 amb[:, 0:128], avb[:, 0:128], aab[:, 0:128],
                                 B["amb"], B["avb"], B["aab"])
                    op("vector", lambda h: h.tensor_sub(
                           atscr[:, :, 0:128], x1[:, :, 0, 0:128],
                           amb[:, None, 0:128].broadcast_to([128, CH, 128])),
                       reads=[x1B, B["amb"]], writes=[B["atscr"]])
                    op("vector", lambda h: h.tensor_mul(
                           h2[:, :, 0:128], atscr[:, :, 0:128],
                           aab[:, None, 0:128].broadcast_to([128, CH, 128])),
                       reads=[B["atscr"], B["aab"]], writes=[B["h2"]])
                    op("vector", lambda h: h.tensor_copy(h2c[:, :, 0:1],
                                                         h2[:, :, 127:128]),
                       reads=[B["h2"]], writes=[B["h2c"]])
                yield

            def gen_ffn(i):
                par = i % 2
                t0 = i * TM
                x1v = x1[:, :, par, :]
                x1B = B["x10"] if par == 0 else B["x11"]

                mxv = mx[:, :, par, :]
                mxB2 = B["mx0"] if par == 0 else B["mx1"]
                op("scalar", lambda h, x1v=x1v: h.activation(
                       sqf[:, :, :], x1v, AF.Square),
                   reads=[x1B], writes=[B["sqf"]])
                yield
                ln_stats_ops([mxv[:, kc, :] for kc in range(CH)],
                             [sqf[:, kc, :] for kc in range(CH)],
                             mxB2, B["sqf"],
                             p3[:, 0:TM], B["p3"], p3[:, TM:2 * TM], B["p3"],
                             mbuf[:, :], vbuf[:, :], abuf[:, :],
                             B["mbuf"], B["vbuf"], B["abuf"])
                yield
                op("vector", lambda h, x1v=x1v: h.tensor_sub(
                       sqf[:, :, :], x1v,
                       mbuf[:, None, :].broadcast_to([128, CH, TM])),
                   reads=[x1B, B["mbuf"]], writes=[B["sqf"]])
                op("vector", lambda h: h.tensor_mul(
                       h2[:, :, :], sqf[:, :, :],
                       abuf[:, None, :].broadcast_to([128, CH, TM])),
                   reads=[B["sqf"], B["abuf"]], writes=[B["h2"]])
                yield

                op("vector", lambda h: h.tensor_copy(h2p[:, :, 1:TM],
                                                     h2[:, :, 0:TM - 1]),
                   reads=[B["h2"]], writes=[B["h2p"]])
                op("gpsimd", lambda h, par=par: h.tensor_copy(
                       h2p[:, :, 0:1], h2c[:, :, par:par + 1]),
                   reads=[B["h2c"]], writes=[B["h2p"]])
                op("gpsimd", lambda h, i=i: h.tensor_copy(
                       h2c[:, :, (i + 1) % 2:(i + 1) % 2 + 1],
                       h2[:, :, TM - 1:TM]),
                   reads=[B["h2"]], writes=[B["h2c"]])
                op("vector", lambda h: h.tensor_sub(sqf[:, :, :], h2[:, :, :],
                                                    h2p[:, :, :]),
                   reads=[B["h2"], B["h2p"]], writes=[B["sqf"]])
                yield
                murb = mur[:, :, None].broadcast_to([128, CH, TM])
                mukb = muk[:, :, None].broadcast_to([128, CH, TM])
                op("vector", lambda h, murb=murb: h.tensor_mul(
                       xrb[:, :, :], sqf[:, :, :], murb),
                   reads=[B["sqf"], B["mur"]], writes=[B["xrb"]])
                op("vector", lambda h: h.tensor_add(xrb[:, :, :], xrb[:, :, :],
                                                    h2p[:, :, :]),
                   reads=[B["h2p"], B["xrb"]], writes=[B["xrb"]])
                op("vector", lambda h, mukb=mukb: h.tensor_mul(
                       xk[:, :, :], sqf[:, :, :], mukb),
                   reads=[B["sqf"], B["muk"]], writes=[B["xk"]])
                op("vector", lambda h: h.tensor_add(xk[:, :, :], xk[:, :, :],
                                                    h2p[:, :, :]),
                   reads=[B["h2p"], B["xk"]], writes=[B["xk"]])
                yield

                def proj_f(wname, src, srcB, og):
                    def mm(h, og=og, src=src):
                        inst = None
                        for j, oc in enumerate(range(4 * og, 4 * og + 4)):
                            for kc in range(CH):
                                inst = h.matmul(
                                    p3[:, j * TM:(j + 1) * TM],
                                    wmain[wname][:, kc,
                                                 oc * 128:(oc + 1) * 128],
                                    src[:, kc, :],
                                    start=(kc == 0), stop=(kc == CH - 1))
                        return inst
                    op("tensor", mm, reads=[B["wmain"], srcB], writes=[B["p3"]])

                for og in range(2):
                    proj_f("wr", xrb, B["xrb"], og)
                    op("scalar", lambda h, og=og: h.activation(
                           tscr[:, 4 * og:4 * og + 4, :], slots(p3, TM, 4),
                           AF.Exp, scale=c_n1),
                       reads=[B["p3"], B["consts"]], writes=[B["tscr"]])
                    yield
                op("scalar", lambda h: h.activation(
                       tscr[:, :, :], tscr[:, :, :], AF.Ln, bias=c_one),
                   reads=[B["tscr"], B["consts"]], writes=[B["tscr"]])
                op("scalar", lambda h: h.activation(
                       rbuf[:, :, :], tscr[:, :, :], AF.Exp, scale=c_n1),
                   reads=[B["tscr"], B["consts"]], writes=[B["rbuf"]])
                yield
                for og, psx, psxB in ((0, p2, B["p2"]), (1, p3, B["p3"])):
                    def wk2mm(h, og=og, psx=psx):
                        inst = None
                        for j, oc in enumerate(range(4 * og, 4 * og + 4)):
                            for kc in range(CH):
                                inst = h.matmul(
                                    psx[:, j * TM:(j + 1) * TM],
                                    wmain["wk2"][:, kc,
                                                 oc * 128:(oc + 1) * 128],
                                    xk[:, kc, :],
                                    start=(kc == 0), stop=(kc == CH - 1))
                        return inst
                    op("tensor", wk2mm, reads=[B["wmain"], B["xk"]],
                       writes=[psxB])
                    op("scalar", lambda h, og=og, psx=psx: h.activation(
                           sqf[:, 4 * og:4 * og + 4, :], slots(psx, TM, 4),
                           AF.Relu),
                       reads=[psxB], writes=[B["sqf"]])
                    yield
                op("scalar", lambda h: h.activation(kact[:, :, :],
                                                    sqf[:, :, :], AF.Square),
                   reads=[B["sqf"]], writes=[B["kact"]])
                yield
                for og in range(2):
                    proj_f("wv2", kact, B["kact"], og)
                    op("vector", lambda h, og=og: h.tensor_mul(
                           tscr[:, 4 * og:4 * og + 4, :],
                           rbuf[:, 4 * og:4 * og + 4, :], slots(p3, TM, 4)),
                       reads=[B["p3"], B["rbuf"]], writes=[B["tscr"]])
                    yield
                op("scalar", lambda h: h.activation(sqf[:, :, :],
                                                    tscr[:, :, :], AF.Square),
                   reads=[B["tscr"]], writes=[B["sqf"]])
                yield
                ln_stats_ops([tscr[:, kc, :] for kc in range(CH)],
                             [sqf[:, kc, :] for kc in range(CH)],
                             B["tscr"], B["sqf"],
                             p3[:, 0:TM], B["p3"], p3[:, TM:2 * TM], B["p3"],
                             mbuf[:, :], vbuf[:, :], abuf[:, :],
                             B["mbuf"], B["vbuf"], B["abuf"])
                yield
                op("vector", lambda h: h.tensor_sub(
                       rbuf[:, :, :], tscr[:, :, :],
                       mbuf[:, None, :].broadcast_to([128, CH, TM])),
                   reads=[B["tscr"], B["mbuf"]], writes=[B["rbuf"]])
                op("vector", lambda h: h.tensor_mul(
                       tscr[:, :, :], rbuf[:, :, :],
                       abuf[:, None, :].broadcast_to([128, CH, TM])),
                   reads=[B["rbuf"], B["abuf"]], writes=[B["tscr"]])
                op("vector", lambda h, x1v=x1v: h.tensor_add(
                       fstage[:, :, 0:128], x1v[:, :, 0:128],
                       tscr[:, :, 0:128]),
                   reads=[B["tscr"], x1B], writes=[B["fstage0"]])
                op("vector", lambda h, x1v=x1v: h.tensor_add(
                       fstage[:, :, 128:TM], x1v[:, :, 128:TM],
                       tscr[:, :, 128:TM]),
                   reads=[B["tscr"], x1B], writes=[B["fstage1"]])
                yield

                for tb in range(2):
                    def trp(h, tb=tb):
                        inst = None
                        for ck in range(CH):
                            inst = h.transpose(
                                p3[:, ck * 128:(ck + 1) * 128],
                                fstage[:, ck, tb * 128:(tb + 1) * 128],
                                ident32[:, :])
                        return inst
                    op("tensor", trp,
                       reads=[B["fstage0"] if tb == 0 else B["fstage1"],
                              B["ident32"]],
                       writes=[B["p3"]])
                    op("vector", lambda h, tb=tb: h.tensor_copy(
                           out_sb[:, tb, :], p3[:, :]),
                       reads=[B["p3"]], writes=[B["out_sb0" if tb == 0
                                                 else "out_sb1"]])
                    r0 = t0 + tb * 128
                    op("gpsimd", lambda h, r0=r0, tb=tb: h.dma_start(
                           out=y_h.ap()[r0:r0 + 128, :], in_=out_sb[:, tb, :]),
                       reads=[B["out_sb0" if tb == 0 else "out_sb1"]],
                       sem="ST0" if tb == 0 else "ST1", dma=True)
                    yield

            def drain(g):
                for _ in g:
                    pass

            def zipper(a, f):
                while True:
                    da = next(a, "end") if a is not None else "end"
                    df = next(f, "end") if f is not None else "end"
                    if da == "end" and df == "end":
                        break

            op("vector", lambda h: h.tensor_copy(Sb[:, :, :], Sf[:, :, :]),
               reads=[B["Sf"]], writes=[B["Sb"]])
            if do_mini:
                drain(gen_attn(0, mini=True))

            if V2:
                pay = fstage.ap().rearrange("p k t -> p (k t)")[:, 0:1032]
                payB = [B["fstage0"], B["fstage1"]]
                h2cf = h2c.ap().rearrange("p k t -> p (k t)")
                op("vector", lambda h: h.tensor_scalar(
                       out=fstage.ap().rearrange("p k t -> p (k t)")[:, 0:1024],
                       in0=Sf.ap().rearrange("p h d -> p (h d)"),
                       scalar1=sflag[:, 0:1], scalar2=None, op0=ALU.mult),
                   reads=[B["Sf"], B["flags"]], writes=payB)
                op("vector", lambda h: h.tensor_scalar(
                       out=fstage.ap().rearrange("p k t -> p (k t)")[:, 1024:1032],
                       in0=h2cf[:, 0:16:2],
                       scalar1=sflag[:, 0:1], scalar2=None, op0=ALU.mult),
                   reads=[B["h2c"], B["flags"]], writes=payB)
                ev1 = op("gpsimd", lambda h: h.dma_start(out=cc_in.ap(),
                                                         in_=pay),
                         reads=payB, sem="CCI", dma=True)
                ccB = buf("cc")
                op("gpsimd", lambda h: h.collective_compute(
                       "AllReduce", ALU.add,
                       replica_groups=[[0, 1], [2, 3], [4, 5], [6, 7]],
                       ins=[cc_in.ap()], outs=[cc_out.ap()]),
                   extra=[ev1], writes=[ccB], sem="CC")
                op("sync", lambda h: h.dma_start(out=pay, in_=cc_out.ap()),
                   reads=[ccB], writes=payB, sem="CC2", dma=True)
                op("vector", lambda h: h.tensor_scalar(
                       out=Sf.ap().rearrange("p h d -> p (h d)"),
                       in0=fstage.ap().rearrange("p k t -> p (k t)")[:, 0:1024],
                       scalar1=uflag[:, 0:1], scalar2=None, op0=ALU.mult),
                   reads=payB + [B["flags"]], writes=[B["Sf"]])
                op("vector", lambda h: h.tensor_scalar(
                       out=h2cf[:, 0:16:2],
                       in0=fstage.ap().rearrange("p k t -> p (k t)")[:, 1024:1032],
                       scalar1=uflag[:, 0:1], scalar2=None, op0=ALU.mult),
                   reads=payB + [B["flags"]], writes=[B["h2c"]])
                op("vector", lambda h: h.tensor_copy(Sb[:, :, :], Sf[:, :, :]),
                   reads=[B["Sf"]], writes=[B["Sb"]])

            NMR = NM if n_main is None else min(n_main, NM)
            for i in range(NMR):
                zipper(gen_attn(i), gen_ffn(i - 1) if i > 0 else None)
            if NMR > 0:
                drain(gen_ffn(NMR - 1))

            if dbg:
                for src, dst in (("s_ln1", "d_ln1"), ("s_kf", "d_kf"),
                                 ("s_kt", "d_kt"), ("s_v", "d_v")):
                    op("sync", lambda h, src=src, dst=dst: h.dma_start(
                           out=d_outs[dst].ap(),
                           in_=(s_ln1 if src == "s_ln1" else
                                s_kf if src == "s_kf" else
                                s_kt if src == "s_kt" else s_v).ap()),
                       reads=[B[src]], sem="DBG", dma=True)
                op("sync", lambda h: h.dma_start(
                       out=d_outs["d_sf"].ap(),
                       in_=Sf.ap().rearrange("p h d -> p (h d)")),
                   reads=[B["Sf"]], sem="DBG", dma=True)
                op("gpsimd", lambda h: h.dma_start(
                       out=d_outs["d_h2c"].ap(),
                       in_=h2c.ap().rearrange("p k t -> p (k t)")),
                   reads=[B["h2c"]], sem="DBG2", dma=True)
            op("sync", lambda h: h.nop(),
               extra=[sch.last["ST0"], sch.last["ST1"], sch.last["DBG"],
                      sch.last["DBG2"]])
            with nc.Block("main") as blk:
                sch.flush(blk)

    return nc


_V2 = True


def kernel(**inp):
    x = np.asarray(inp["L0_attn_hidden_states"])
    Bn, T, C = x.shape
    HALF = T // 2
    nc = build_program(HALF=HALF, V2=_V2)

    wmap = {
        "wq": "L0_attn_Wq", "wk": "L0_attn_Wk", "wv": "L0_attn_Wv",
        "wo": "L0_attn_Wo", "wr": "L0_ffn_W_r", "wk2": "L0_ffn_W_k",
        "wv2": "L0_ffn_W_v", "fmq1": "L0_attn_fmq_w1",
        "fmq2": "L0_attn_fmq_w2", "fmk1": "L0_attn_fmk_w1",
        "fmk2": "L0_attn_fmk_w2",
    }
    base = {}
    for k, src in wmap.items():
        base[k] = np.ascontiguousarray(np.asarray(inp[src]).T.astype(np.float32))
    for k, src in (("fbq1", "L0_attn_fmq_b1"), ("fbq2", "L0_attn_fmq_b2"),
                   ("fbk1", "L0_attn_fmk_b1"), ("fbk2", "L0_attn_fmk_b2"),
                   ("rmsw", "L0_attn_rms_w")):
        base[k] = np.ascontiguousarray(np.asarray(inp[src], dtype=np.float32))
    base["mur"] = np.ascontiguousarray(
        np.asarray(inp["L0_ffn_mu_r"], dtype=np.float32).reshape(CH, 128).T)
    base["muk"] = np.ascontiguousarray(
        np.asarray(inp["L0_ffn_mu_k"], dtype=np.float32).reshape(CH, 128).T)
    base["ident"] = np.eye(128, dtype=np.float32)
    base["trimask"] = np.ascontiguousarray(np.concatenate(
        [np.triu(np.ones((128, 128), np.float32)),
         np.ones((128, 128), np.float32)], axis=1))

    in_maps = []
    for c in range(8):
        b, hf = c // 2, c % 2
        own = x[b, hf * HALF:(hf + 1) * HALF]
        m = dict(base)
        if _V2:
            m["xkv"] = np.ascontiguousarray(own.T.astype(np.float32))
            m["flags"] = np.ascontiguousarray(np.broadcast_to(
                np.array([1.0, 0.0] if hf == 0 else [0.0, 1.0], np.float32),
                (128, 2)))
        else:
            pre = np.zeros((HALF, C), np.float32) if hf == 0 else x[b, :HALF]
            m["xkv"] = np.ascontiguousarray(
                np.concatenate([pre, own], axis=0).T.astype(np.float32))
        in_maps.append(m)

    res = run_bass_kernel_spmd(nc, in_maps, core_ids=list(range(8)))
    out = np.empty((Bn, T, C), np.float32)
    for c in range(8):
        b, hf = c // 2, c % 2
        out[b, hf * HALF:(hf + 1) * HALF] = res.results[c]["y"]
    return out
